# version 04
# baseline (speedup 1.0000x reference)
import numpy as np
from contextlib import ExitStack

import concourse.bass as bass
import concourse.mybir as mybir
from concourse.bass_utils import run_bass_kernel_spmd

F32 = mybir.dt.float32
BF16 = mybir.dt.bfloat16
I32 = mybir.dt.int32
F16 = mybir.dt.float16
AF = mybir.ActivationFunctionType
ALU = mybir.AluOpType
AX = mybir.AxisListType

P = 128
D = 2048
S = 2048
NC = D // P
H = 16
HD = 128
M0 = 16384
F1 = 8192
R = 16
SCALE_ATT = float(1.0 / np.sqrt(128.0))
NT = S // 512

PI = np.concatenate([np.arange(0, D, 2), np.arange(1, D, 2)])
SIG = np.concatenate([np.arange(0, M0, 2), np.arange(1, M0, 2)])


def _ct(a):
    return np.ascontiguousarray(a)



def prep_weights(ins):
    w = {}
    pi = PI
    gp = np.asarray(ins["L0_ffn_gate_W_packed"])
    up = np.asarray(ins["L0_ffn_up_W_packed"])
    w["gup_packedT"] = _ct(np.stack([gp[SIG].T, up[SIG].T]).reshape(2, 8, P, M0))
    gs = np.asarray(ins["L0_ffn_gate_scales"])
    us = np.asarray(ins["L0_ffn_up_scales"])
    w["gup_scalesT"] = _ct(np.stack([gs[SIG].T, us[SIG].T]))
    dp = np.asarray(ins["L0_ffn_down_W_packed"])
    w["down_packedT"] = _ct(dp[pi].T.reshape(64, P, D))
    ds = np.asarray(ins["L0_ffn_down_scales"])
    w["down_scalesT"] = _ct(ds[pi].T)

    def prepT(a):
        return _ct(np.asarray(a).T[pi].reshape(NC, P, np.asarray(a).shape[0]))

    def dd_head_tile0(a):
        return (np.asarray(a).T[pi].reshape(NC, P, H, HD)
                .transpose(2, 1, 0, 3).reshape(H, P, NC * HD))

    w["a0_qkv"] = _ct(np.concatenate(
        [dd_head_tile0(ins[f"L0_attn_W{s}"]) for s in "qkv"], axis=2))
    w["a0_wo"] = _ct(np.asarray(ins["L0_attn_Wo"])[pi, :].T.reshape(NC, P, D))

    def head_tile(a):
        return a.reshape(H, NC, P, HD).transpose(0, 2, 1, 3).reshape(H, P, NC * HD)

    wq1 = np.asarray(ins["L1_attn_W_Q"]).transpose(0, 2, 1)[:, pi, :]
    wk1 = np.asarray(ins["L1_attn_W_K"]).transpose(0, 2, 1)[:, pi, :]
    wv1 = np.asarray(ins["L1_attn_W_V"]).transpose(0, 2, 1)[:, pi, :]
    wo1 = np.asarray(ins["L1_attn_W_O"])
    w["a1_qkv"] = _ct(np.concatenate([head_tile(wq1), head_tile(wk1),
                                      head_tile(wv1)], axis=2))
    w["a1_wo"] = _ct(wo1[:, pi, :].transpose(0, 2, 1))

    def dd_head_tile(a):
        return (np.asarray(a).T[pi].reshape(NC, P, H, HD)
                .transpose(2, 1, 0, 3).reshape(H, P, NC * HD))
    w["f1_qkv"] = _ct(np.concatenate(
        [dd_head_tile(ins[f"L1_ffn_W{s}"]) for s in "qkv"], axis=2))
    w["f1_wo"] = _ct(np.asarray(ins["L1_ffn_Wo"])[pi, :].T.reshape(NC, P, D))
    w["f1_a"] = _ct(np.asarray(ins["L1_ffn_A"])[pi].reshape(NC, P, R)
                    .transpose(1, 0, 2).reshape(P, NC * R))
    w["f1_b"] = _ct(np.asarray(ins["L1_ffn_B"]))
    def jtile(a):
        return _ct(np.asarray(a).T[pi].reshape(NC, P, F1 // P, P)
                   .transpose(2, 1, 0, 3).reshape(F1 // P, P, NC * P))
    w["f1_win"] = jtile(ins["L1_ffn_W_in_base"])
    w["f1_wup"] = jtile(ins["L1_ffn_W_up_base"])
    w["f1_wdn"] = _ct(np.asarray(ins["L1_ffn_W_down"])[pi, :].T.reshape(64, P, D))
    w["f1_ln1"] = _ct(np.asarray(ins["L1_ffn_ln1_w"])[pi].reshape(NC, P).T)
    w["f1_ln2"] = _ct(np.asarray(ins["L1_ffn_ln2_w"])[pi].reshape(NC, P).T)

    w["ones_col"] = np.ones((P, 1), np.float32)
    w["ones_row"] = np.ones((1, P), np.float32)
    w["ident"] = np.eye(P, dtype=np.float32)
    masks = np.zeros((4, P, 512), np.float32)
    for m in range(4):
        masks[m] = (m * 128 + np.arange(P)[:, None]) <= np.arange(512)[None, :]
    w["diag_masks"] = masks
    w["triu64"] = _ct(np.tile(np.triu(np.ones((64, 64), np.float32)), (2, 1)))
    return w


def prep_x(x_full, core):
    return _ct(np.asarray(x_full)[core].T[PI].reshape(NC, P, S))


def post_out(x_out):
    inv = np.argsort(PI)
    return _ct(np.asarray(x_out, np.float32).reshape(D, S)[inv].T)



class Sem:
    def __init__(self, ctx, nc, name):
        self.h = ctx.enter_context(nc.semaphore(name))
        self.n = 0

    def inc(self, instr, k=1):
        instr.then_inc(self.h, k)
        self.n += k

    def dma(self, instr):
        self.inc(instr, 16)

    def wait(self, eng, n=None):
        eng.wait_ge(self.h, self.n if n is None else n)


class Prog:
    def __init__(self, nc, ctx):
        self.nc = nc
        self.ctx = ctx
        self._stack = [ctx]
        self.te = nc.tensor
        self.ve = nc.vector
        self.se = nc.scalar
        self.po = nc.gpsimd
        self.sy = nc.sync
        for name in ["ldx", "lds", "ldg", "mm", "act", "vec", "pool", "sto", "sto2"]:
            setattr(self, "s_" + name, Sem(ctx, nc, name))
        self.uid = 0

    def push(self):
        st = ExitStack()
        self._stack.append(st)
        return st

    def pop(self):
        self._stack.pop().close()

    def _c(self):
        return self._stack[-1]

    def name(self, base):
        self.uid += 1
        return f"{base}{self.uid}"

    def sb(self, shape, dt):
        return self._c().enter_context(self.nc.sbuf_tensor(self.name("t"), shape, dt))

    def psum(self, shape, dt=F32):
        return self._c().enter_context(self.nc.psum_tensor(self.name("q"), shape, dt))

    def bar(self):
        self.nc.all_engine_barrier()


def emit_const_load(p, g):
    c = {}
    c["ones_col_f"] = p.sb([P, 1], F32)
    c["ones_row_f"] = p.sb([1, P], F32)
    c["ones_row_b"] = p.sb([1, P], BF16)
    c["ones_col_b"] = p.sb([P, 1], BF16)
    c["ident_b"] = p.sb([P, P], BF16)
    c["masks_b"] = p.sb([P, 4 * 512], BF16)
    c["triu64_b"] = p.sb([P, 64], BF16)
    p.s_ldx.dma(p.sy.dma_start(out=c["ones_col_f"][:], in_=g["ones_col"]))
    p.s_ldx.dma(p.sy.dma_start(out=c["ones_row_f"][:], in_=g["ones_row"]))
    p.s_ldg.dma(p.po.dma_start(out=c["ones_col_b"][:], in_=g["ones_col"]))
    p.s_ldg.dma(p.po.dma_start(out=c["ones_row_b"][:], in_=g["ones_row"]))
    p.s_ldg.dma(p.po.dma_start(out=c["ident_b"][:], in_=g["ident"]))
    p.s_ldg.dma(p.po.dma_start(
        out=c["masks_b"][:].rearrange("p (m f) -> p m f", m=4),
        in_=g["diag_masks"].transpose([1, 0, 2])))
    p.s_ldg.dma(p.po.dma_start(out=c["triu64_b"][:], in_=g["triu64"]))
    p.bar()
    p.s_ldx.wait(p.ve)
    p.s_ldg.wait(p.ve)
    p.bar()
    return c


def emit_ln(p, c, x_dram, y_sb, t0, tw, rstd_r=None, x_keep=None, y32_dram=None):
    p.push()
    xb = x_keep if x_keep is not None else p.sb([P, NC * 512], F32)
    sqb = p.sb([P, 2 * 512], F32)
    ps_stat = p.psum([1, 2 * 512])
    ps_bc = p.psum([P, 2 * 512])
    rowbuf = p.sb([1, 6 * 512], F32)
    tmp = p.sb([P, 512], F32)
    tmp32 = p.sb([P, 2 * 512], F32) if y32_dram is not None else None
    sto_after = []
    for st in range(tw // 512):
        tt = t0 + st * 512

        def xt(ch):
            if x_keep is not None:
                return xb[:, ch * tw + st * 512: ch * tw + st * 512 + 512]
            return xb[:, ch * 512:(ch + 1) * 512]

        for ch in range(NC):
            p.s_ldx.dma(p.sy.dma_start(out=xt(ch), in_=x_dram[ch, :, tt:tt + 512]))
        p.bar()
        p.s_ldx.wait(p.te)
        p.s_ldx.wait(p.se)
        mm_after = []
        for ch in range(NC):
            sq = sqb[:, (ch % 2) * 512:(ch % 2) * 512 + 512]
            p.te.matmul(out=ps_stat[:, 0:512], lhsT=c["ones_col_f"][:], rhs=xt(ch),
                        start=(ch == 0), stop=(ch == NC - 1))
            if ch >= 2:
                p.s_mm.wait(p.se, mm_after[ch - 2])
            p.s_act.inc(p.se.activation(out=sq, in_=xt(ch), func=AF.Square))
            p.s_act.wait(p.te)
            ins = p.te.matmul(out=ps_stat[:, 512:1024], lhsT=c["ones_col_f"][:],
                              rhs=sq, start=(ch == 0), stop=(ch == NC - 1))
            p.s_mm.inc(ins)
            mm_after.append(p.s_mm.n)
        p.s_mm.wait(p.ve)
        mean = rowbuf[:, 0:512]
        var = rowbuf[:, 512:1024]
        t2 = rowbuf[:, 1024:1536]
        rstd = rowbuf[:, 1536:2048]
        cc = rowbuf[:, 2048:2560]
        t5 = rowbuf[:, 2560:3072]
        p.s_vec.inc(p.ve.tensor_scalar(out=mean, in0=ps_stat[:, 0:512],
                                       scalar1=1.0 / D, scalar2=None, op0=ALU.mult))
        p.s_vec.inc(p.ve.tensor_scalar(out=var, in0=ps_stat[:, 512:1024],
                                       scalar1=1.0 / D, scalar2=None, op0=ALU.mult))
        p.s_vec.wait(p.ve)
        p.s_vec.inc(p.ve.tensor_tensor(out=t2, in0=mean, in1=mean, op=ALU.mult))
        p.s_vec.wait(p.ve)
        p.s_vec.inc(p.ve.tensor_tensor(out=var, in0=var, in1=t2, op=ALU.subtract))
        p.s_vec.wait(p.ve)
        p.s_vec.inc(p.ve.tensor_scalar(out=t2, in0=var, scalar1=1e-5,
                                       scalar2=None, op0=ALU.add))
        p.s_vec.wait(p.se)
        p.s_act.inc(p.se.activation(out=t2, in_=t2, func=AF.Sqrt))
        p.s_act.wait(p.ve)
        p.s_vec.inc(p.ve.reciprocal(out=rstd, in_=t2))
        p.s_vec.wait(p.ve)
        p.s_vec.inc(p.ve.tensor_tensor(out=t2, in0=mean, in1=rstd, op=ALU.mult))
        p.s_vec.wait(p.ve)
        p.s_vec.inc(p.ve.tensor_scalar(out=cc, in0=t2, scalar1=-1.0,
                                       scalar2=None, op0=ALU.mult))
        if rstd_r is not None:
            rr = rstd_r[:, st * 512:(st + 1) * 512]
            p.s_vec.inc(p.ve.tensor_scalar(out=t5, in0=var, scalar1=1e-5,
                                           scalar2=None, op0=ALU.add))
            p.s_vec.wait(p.ve)
            p.s_vec.inc(p.ve.reciprocal(out=t5, in_=t5))
            p.s_vec.wait(p.ve)
            p.s_vec.inc(p.ve.tensor_tensor(out=t5, in0=var, in1=t5, op=ALU.mult))
            p.s_vec.wait(p.ve)
            p.s_vec.inc(p.ve.tensor_scalar(out=t5, in0=t5, scalar1=1e-6,
                                           scalar2=None, op0=ALU.add))
            p.s_vec.wait(p.se)
            p.s_act.inc(p.se.activation(out=t5, in_=t5, func=AF.Sqrt))
            p.s_act.wait(p.ve)
            p.s_vec.inc(p.ve.reciprocal(out=rr, in_=t5))
        p.s_vec.wait(p.te)
        p.te.matmul(out=ps_bc[:, 0:512], lhsT=c["ones_row_f"][:], rhs=rstd,
                    start=True, stop=True)
        p.s_mm.inc(p.te.matmul(out=ps_bc[:, 512:1024], lhsT=c["ones_row_f"][:],
                               rhs=cc, start=True, stop=True))
        p.s_mm.wait(p.ve)
        for ch in range(NC):
            p.s_vec.inc(p.ve.tensor_tensor(out=tmp[:], in0=xt(ch),
                                           in1=ps_bc[:, 0:512], op=ALU.mult))
            p.s_vec.wait(p.ve)
            if y32_dram is not None:
                y32t = tmp32[:, (ch % 2) * 512:(ch % 2) * 512 + 512]
                if len(sto_after) >= 2:
                    p.s_sto.wait(p.ve, sto_after[-2])
                p.s_vec.inc(p.ve.tensor_tensor(out=y32t, in0=tmp[:],
                                               in1=ps_bc[:, 512:1024], op=ALU.add))
                p.s_vec.wait(p.sy)
                p.s_sto.dma(p.sy.dma_start(
                    out=y32_dram[ch, :, tt:tt + 512], in_=y32t))
                sto_after.append(p.s_sto.n)
                if y_sb is not None:
                    ins = p.ve.tensor_scalar(
                        out=y_sb[:, ch * tw + st * 512: ch * tw + st * 512 + 512],
                        in0=y32t, scalar1=1.0, scalar2=None, op0=ALU.mult)
                    p.s_vec.inc(ins)
            else:
                ins = p.ve.tensor_tensor(
                    out=y_sb[:, ch * tw + st * 512: ch * tw + st * 512 + 512],
                    in0=tmp[:], in1=ps_bc[:, 512:1024], op=ALU.add)
                if ch == NC - 1:
                    p.s_vec.inc(ins)
        p.bar()
    p.pop()



def _dequant_tile(p, eng, vi, s_eff, out_lo, out_hi, scr):
    F_ = vi.shape[1]
    vh, h32, mag, sg2 = (scr[k] for k in ["vh", "h32", "mag", "sg2"])
    eng.tensor_scalar(out=h32[:, :F_], in0=vi, scalar1=7, scalar2=25,
                      op0=ALU.bitwise_and, op1=ALU.logical_shift_left)
    eng.tensor_tensor(out=mag[:, :F_], in0=h32[:, :F_].bitcast(F16)[:, 1::2],
                      in1=s_eff, op=ALU.mult)
    eng.tensor_scalar(out=sg2[:, :F_], in0=vi, scalar1=2, scalar2=2,
                      op0=ALU.logical_shift_right, op1=ALU.bitwise_and)
    eng.scalar_tensor_tensor(out=out_lo, in0=sg2[:, :F_], scalar=1.0,
                             in1=mag[:, :F_], op0=ALU.subtract, op1=ALU.mult)
    eng.tensor_scalar(out=vh[:, :F_], in0=vi, scalar1=4, scalar2=None,
                      op0=ALU.logical_shift_right)
    eng.tensor_scalar(out=h32[:, :F_], in0=vh[:, :F_], scalar1=7, scalar2=25,
                      op0=ALU.bitwise_and, op1=ALU.logical_shift_left)
    eng.tensor_tensor(out=mag[:, :F_], in0=h32[:, :F_].bitcast(F16)[:, 1::2],
                      in1=s_eff, op=ALU.mult)
    eng.tensor_scalar(out=sg2[:, :F_], in0=vh[:, :F_], scalar1=2, scalar2=2,
                      op0=ALU.logical_shift_right, op1=ALU.bitwise_and)
    return eng.scalar_tensor_tensor(out=out_hi, in0=sg2[:, :F_], scalar=1.0,
                                    in1=mag[:, :F_], op0=ALU.subtract, op1=ALU.mult)


def emit_ffn0_dequant(p, g, cache):
    st_ = p.push()
    MS = 2048
    lanes = []
    for li, (eng, qsem, ldq, ldsem, stq, stsem) in enumerate([
            (p.ve, p.s_vec, p.sy, p.s_ldx, p.sy, p.s_sto),
            (p.ve, p.s_pool, p.se, p.s_lds, p.se, p.s_sto2)]):
        scr = {k: p.sb([P, MS], I32) for k in ["vh", "h32", "sg2"]}
        scr["mag"] = p.sb([P, MS], F32)
        lanes.append(dict(
            eng=eng, qsem=qsem, ldq=ldq, ldsem=ldsem, stq=stq, stsem=stsem,
            scr=scr, vi=[p.sb([P, MS], I32) for _ in range(2)],
            sl=[p.sb([P, MS], F32) for _ in range(2)],
            ol=[p.sb([P, MS], BF16) for _ in range(2)],
            oh=[p.sb([P, MS], BF16) for _ in range(2)],
            i=0))

    jobs = []
    nj = MS // P
    for mat, wname in ((0, "wg"), (1, "wu")):
        for kc in range(8):
            for ms in range(0, M0, MS):
                j0 = ms // P
                dlo = cache[wname][j0:j0 + nj, :, kc * P:(kc + 1) * P]\
                    .transpose([1, 0, 2])
                dhi = cache[wname][j0:j0 + nj, :, (kc + 8) * P:(kc + 9) * P]\
                    .transpose([1, 0, 2])
                jobs.append((g["gup_packedT"][mat, kc, :, ms:ms + MS],
                             g["gup_scalesT"][mat, 8 * kc:8 * kc + 8, ms:ms + MS],
                             dlo, dhi))
    for mc in range(64):
        jobs.append((g["down_packedT"][mc], g["down_scalesT"][8 * mc:8 * mc + 8, :],
                     cache["wd"][mc], cache["wd"][mc + 64]))

    for gi, (src, srows, dlo, dhi) in enumerate(jobs):
        L = lanes[gi % 2]
        i, b = L["i"], L["i"] % 2
        eng = L["eng"]
        Fw = src.shape[-1]
        if i >= 2:
            L["ldq"].wait_ge(L["qsem"].h, i - 1)
        L["ldsem"].dma(L["ldq"].dma_start(out=L["vi"][b][:, :Fw], in_=src))
        L["ldsem"].dma(L["ldq"].dma_start(
            out=L["sl"][b][:, :Fw],
            in_=srows.unsqueeze(1).broadcast_to([8, 16, Fw])))
        L["ldsem"].wait(eng)
        if i >= 2:
            L["stsem"].wait(eng, 32 * (i - 1))
        eng.tensor_scalar(out=L["sl"][b][:, :Fw], in0=L["sl"][b][:, :Fw],
                          scalar1=-16384.0, scalar2=None, op0=ALU.mult)
        ins = _dequant_tile(p, eng, L["vi"][b][:, :Fw], L["sl"][b][:, :Fw],
                            L["ol"][b][:, :Fw], L["oh"][b][:, :Fw], L["scr"])
        L["qsem"].inc(ins)
        L["qsem"].wait(L["stq"])
        L["stsem"].dma(L["stq"].dma_start(out=dlo, in_=L["ol"][b][:, :Fw]))
        L["stsem"].dma(L["stq"].dma_start(out=dhi, in_=L["oh"][b][:, :Fw]))
        L["i"] += 1
    p.bar()
    p.pop()


def emit_ffn0_compute(p, c, g, cache, x_in, x_out):
    TB = 512
    MSLAB = 16
    NSLAB = M0 // (MSLAB * P)
    for t0 in range(0, S, TB):
        st_ = p.push()
        y = p.sb([P, NC * TB], BF16)
        emit_ln(p, c, x_in, y, t0, TB)
        acc = p.sb([P, NC * TB], F32)
        inter = p.sb([P, MSLAB * TB], BF16)
        wgu_b = [p.sb([P, 2 * NC * P], BF16) for _ in range(2)]
        wd_slab = p.sb([P, MSLAB * D], BF16)
        ps_g = p.psum([P, TB])
        ps_u = p.psum([P, TB])
        ps_d = p.psum([P, 2 * TB])
        gg = p.sb([P, TB], BF16)
        for sl_i in range(NSLAB):
            for jj in range(MSLAB):
                j = sl_i * MSLAB + jj
                p.s_lds.dma(p.se.dma_start(out=wd_slab[:, jj * D:(jj + 1) * D],
                                           in_=cache["wd"][j]))
            mm_after = []
            for jj in range(MSLAB):
                j = sl_i * MSLAB + jj
                b = jj % 2
                if jj >= 2:
                    p.sy.wait_ge(p.s_mm.h, mm_after[jj - 2])
                p.s_ldx.dma(p.sy.dma_start(out=wgu_b[b][:, :NC * P],
                                           in_=cache["wg"][j]))
                p.s_ldx.dma(p.sy.dma_start(out=wgu_b[b][:, NC * P:],
                                           in_=cache["wu"][j]))
                p.s_ldx.wait(p.te)
                p.s_vec.wait(p.te)
                for ch in range(NC):
                    p.te.matmul(out=ps_g[:], lhsT=wgu_b[b][:, ch * P:(ch + 1) * P],
                                rhs=y[:, ch * TB:(ch + 1) * TB],
                                start=(ch == 0), stop=(ch == NC - 1))
                for ch in range(NC):
                    ins = p.te.matmul(out=ps_u[:],
                                      lhsT=wgu_b[b][:, NC * P + ch * P: NC * P + (ch + 1) * P],
                                      rhs=y[:, ch * TB:(ch + 1) * TB],
                                      start=(ch == 0), stop=(ch == NC - 1))
                p.s_mm.inc(ins)
                mm_after.append(p.s_mm.n)
                p.s_mm.wait(p.se)
                p.s_act.inc(p.se.activation(out=gg[:], in_=ps_g[:],
                                            func=AF.Gelu_apprx_tanh))
                p.s_act.wait(p.ve)
                p.s_vec.inc(p.ve.tensor_tensor(out=inter[:, jj * TB:(jj + 1) * TB],
                                               in0=gg[:], in1=ps_u[:], op=ALU.mult))
            p.bar()
            p.s_lds.wait(p.te)
            for hc in range(NC):
                for jj in range(MSLAB):
                    ins = p.te.matmul(out=ps_d[:, 0:TB],
                                      lhsT=wd_slab[:, jj * D + hc * P: jj * D + (hc + 1) * P],
                                      rhs=inter[:, jj * TB:(jj + 1) * TB],
                                      start=(jj == 0), stop=(jj == MSLAB - 1))
                p.s_mm.inc(ins)
                if sl_i == 0:
                    p.s_mm.wait(p.se)
                    p.s_act.inc(p.se.activation(out=acc[:, hc * TB:(hc + 1) * TB],
                                                in_=ps_d[:, 0:TB], func=AF.Copy))
                    p.s_act.wait(p.te)
                else:
                    p.s_mm.wait(p.ve)
                    p.s_vec.inc(p.ve.tensor_tensor(out=acc[:, hc * TB:(hc + 1) * TB],
                                                   in0=acc[:, hc * TB:(hc + 1) * TB],
                                                   in1=ps_d[:, 0:TB], op=ALU.add))
                    p.s_vec.wait(p.te)
            p.bar()
        xb = p.sb([P, 512], F32)
        xo = p.sb([P, 512], F32)
        for ch in range(NC):
            p.s_ldx.dma(p.sy.dma_start(out=xb[:], in_=x_in[ch, :, t0:t0 + TB]))
            p.s_ldx.wait(p.ve)
            p.s_vec.inc(p.ve.scalar_tensor_tensor(out=xo[:], in0=acc[:, ch * TB:(ch + 1) * TB],
                                                  scalar=0.5, in1=xb[:],
                                                  op0=ALU.mult, op1=ALU.add))
            p.s_vec.wait(p.sy)
            p.s_sto.dma(p.sy.dma_start(out=x_out[ch, :, t0:t0 + TB], in_=xo[:]))
            p.s_sto.wait(p.ve)
            p.bar()
        p.pop()
        p.bar()



def emit_weight_prep(p, g, wc):
    p.push()
    stg = [p.sb([P, F1], F32) for _ in range(2)]
    ob = [p.sb([P, F1], BF16) for _ in range(2)]
    jobs = []
    for h in range(H):
        jobs.append((g["a1_qkv"][h], wc["qkv_a1"][h], P, 3 * NC * HD))
        jobs.append((g["f1_qkv"][h], wc["qkv_f1"][h], P, 3 * NC * HD))
    for tag, sn in (("wo_a0", "a0_wo"), ("wo_a1", "a1_wo"), ("wo_f1", "f1_wo")):
        for ci in range(NC):
            jobs.append((g[sn][ci], wc[tag][ci], P, D))
    for tag, sn in (("win", "f1_win"), ("wup", "f1_wup")):
        for j in range(F1 // P):
            jobs.append((g[sn][j], wc[tag][j], P, NC * P))
    for j in range(64):
        jobs.append((g["f1_wdn"][j], wc["wdn"][j], P, D))
    jobs.append((g["f1_a"], wc["a_b"][:], P, NC * R))
    jobs.append((g["f1_b"], wc["b_b"][:], R, F1))

    acts = []
    for i, (src, dst, np_, width) in enumerate(jobs):
        b = i % 2
        q = p.sy if b == 0 else p.se
        lsem = p.s_ldx if b == 0 else p.s_lds
        ssem = p.s_sto if b == 0 else p.s_sto2
        if i >= 2:
            q.wait_ge(p.s_act.h, acts[i - 2])
        lsem.dma(q.dma_start(out=stg[b][:np_, :width], in_=src))
        lsem.wait(p.se)
        p.s_act.inc(p.se.activation(out=ob[b][:np_, :width],
                                    in_=stg[b][:np_, :width], func=AF.Copy))
        acts.append(p.s_act.n)
        p.s_act.wait(q)
        ssem.dma(q.dma_start(out=dst, in_=ob[b][:np_, :width]))
    p.bar()
    p.pop()


def emit_qkv_head(p, qkv_cache, h, y, wbuf, ps, evac):
    p.s_ldx.dma(p.sy.dma_start(out=wbuf[:], in_=qkv_cache[h]))
    p.s_ldx.wait(p.te)
    for mi in range(3):
        for tt in range(NT):
            for ch in range(NC):
                ins = p.te.matmul(
                    out=ps[:],
                    lhsT=wbuf[:, (mi * NC + ch) * HD:(mi * NC + ch + 1) * HD],
                    rhs=y[:, ch * S + tt * 512: ch * S + tt * 512 + 512],
                    start=(ch == 0), stop=(ch == NC - 1))
            p.s_mm.inc(ins)
            evac(mi, tt, ps)


def emit_oproj_residual(p, c, wo_cache, z_dram, x_in, x_out,
                        half=False, y_dram=None, out_att=None):
    p.push()
    zrow = p.sb([P, NC * 512], BF16)
    wo_all = p.sb([P, NC * D], BF16)
    ps_o = p.psum([P, 512])
    xb = p.sb([P, 512], F32)
    yb = p.sb([P, 512], BF16)
    ob = p.sb([P, 512], F32)
    obb = p.sb([P, 512], BF16)
    for ci in range(NC):
        p.s_ldx.dma(p.sy.dma_start(out=wo_all[:, ci * D:(ci + 1) * D],
                                   in_=wo_cache[ci]))
    p.bar()
    p.s_ldx.wait(p.te)
    for tt in range(NT):
        for ch in range(NC):
            p.s_ldx.dma(p.sy.dma_start(out=zrow[:, ch * 512:(ch + 1) * 512],
                                       in_=z_dram[ch, :, tt * 512:(tt + 1) * 512]))
        p.bar()
        p.s_ldx.wait(p.te)
        for co in range(NC):
            for ci in range(NC):
                ins = p.te.matmul(out=ps_o[:],
                                  lhsT=wo_all[:, ci * D + co * P: ci * D + (co + 1) * P],
                                  rhs=zrow[:, ci * 512:(ci + 1) * 512],
                                  start=(ci == 0), stop=(ci == NC - 1))
            p.s_mm.inc(ins)
            if out_att is not None:
                p.s_mm.wait(p.se)
                p.s_act.inc(p.se.activation(out=obb[:], in_=ps_o[:], func=AF.Copy))
                p.s_act.wait(p.sy)
                p.s_sto.dma(p.sy.dma_start(
                    out=out_att[co, :, tt * 512:(tt + 1) * 512], in_=obb[:]))
                p.s_sto.wait(p.se)
                p.s_sto.wait(p.te)
            else:
                p.s_ldx.dma(p.sy.dma_start(out=xb[:],
                                           in_=x_in[co, :, tt * 512:(tt + 1) * 512]))
                if y_dram is not None:
                    p.s_ldx.dma(p.sy.dma_start(
                        out=yb[:], in_=y_dram[co, :, tt * 512:(tt + 1) * 512]))
                p.s_ldx.wait(p.ve)
                p.s_mm.wait(p.ve)
                sc = 0.5 if half else 1.0
                p.s_vec.inc(p.ve.scalar_tensor_tensor(out=ob[:], in0=ps_o[:],
                                                      scalar=sc, in1=xb[:],
                                                      op0=ALU.mult, op1=ALU.add))
                p.s_vec.wait(p.ve)
                if y_dram is not None:
                    p.s_vec.inc(p.ve.scalar_tensor_tensor(
                        out=ob[:], in0=yb[:], scalar=0.5, in1=ob[:],
                        op0=ALU.mult, op1=ALU.add))
                    p.s_vec.wait(p.ve)
                p.s_vec.wait(p.sy)
                p.s_sto.dma(p.sy.dma_start(
                    out=x_out[co, :, tt * 512:(tt + 1) * 512], in_=ob[:]))
                p.s_sto.wait(p.ve)
                p.s_sto.wait(p.te)
            p.bar()
    p.pop()


def emit_att_softmax_heads(p, c, qkv_cache, y, z_dram):
    p.push()
    wbuf = p.sb([P, 3 * NC * HD], BF16)
    ps = p.psum([P, 512])
    qb = p.sb([P, S], BF16)
    kb = p.sb([P, S], BF16)
    vb = p.sb([P, S], BF16)
    vT = p.sb([P, NC * HD], BF16)
    ps_t = p.psum([P, P], BF16)
    ps_sc = [p.psum([P, 512]) for _ in range(2)]
    ps_z = p.psum([P, 512])
    ps_r = p.psum([P, 512])
    ps_n = p.psum([1, 2 * 512])
    probsT = p.sb([P, NC * 512], BF16)
    sq = p.sb([P, 512], F32)
    rows = p.sb([1, 4 * S], F32)
    negB = p.sb([1, S], BF16)
    kap = p.sb([1, 1], F32)
    zsb = p.sb([P, 512], BF16)

    for h in range(H):
        def evac(mi, tt, ps_):
            dst = (qb, kb, vb)[mi]
            p.s_mm.wait(p.ve)
            p.s_vec.inc(p.ve.tensor_copy(out=dst[:, tt * 512:(tt + 1) * 512],
                                         in_=ps_[:]))
            p.s_vec.wait(p.te)

        emit_qkv_head(p, qkv_cache, h, y, wbuf, ps, evac)
        p.bar()
        for sc_ in range(NC):
            p.s_mm.inc(p.te.transpose(out=ps_t[:], in_=vb[:, sc_ * P:(sc_ + 1) * P],
                                      identity=c["ident_b"][:]))
            p.s_mm.wait(p.se)
            p.s_act.inc(p.se.activation(out=vT[:, sc_ * P:(sc_ + 1) * P],
                                        in_=ps_t[:], func=AF.Copy))
            p.s_act.wait(p.te)
        p.bar()
        for which, buf in ((0, qb), (1, kb)):
            for tt in range(NT):
                p.s_act.inc(p.se.activation(out=sq[:],
                                            in_=buf[:, tt * 512:(tt + 1) * 512],
                                            func=AF.Square))
                p.s_act.wait(p.te)
                p.s_mm.inc(p.te.matmul(out=ps_n[:, which * 512:(which + 1) * 512],
                                       lhsT=c["ones_col_f"][:], rhs=sq[:],
                                       start=True, stop=True))
                p.s_mm.wait(p.ve)
                p.s_vec.inc(p.ve.tensor_copy(
                    out=rows[:, which * S + tt * 512: which * S + tt * 512 + 512],
                    in_=ps_n[:, which * 512:(which + 1) * 512]))
                p.s_vec.wait(p.se)
                p.s_vec.wait(p.te)
        p.bar()
        p.s_vec.inc(p.ve.reduce_max(out=kap[:], in_=rows[:, S:2 * S], axis=AX.X))
        p.s_vec.wait(p.ve)
        p.s_vec.inc(p.ve.tensor_scalar(out=kap[:], in0=kap[:], scalar1=1.0,
                                       scalar2=None, op0=ALU.mult))
        p.s_vec.wait(p.se)
        p.s_act.inc(p.se.activation(out=kap[:], in_=kap[:], func=AF.Sqrt))
        p.s_act.inc(p.se.activation(out=rows[:, 0:S], in_=rows[:, 0:S],
                                    func=AF.Sqrt))
        p.s_act.wait(p.ve)
        p.s_vec.inc(p.ve.tensor_scalar(out=negB[:], in0=rows[:, 0:S],
                                       scalar1=kap[:, 0:1], scalar2=-1.0,
                                       op0=ALU.mult, op1=ALU.mult))
        p.s_vec.wait(p.te)
        p.bar()
        for cq in range(NT):
            nsk = 4 * (cq + 1)
            for m in range(nsk):
                pb = m % 2
                p.s_act.wait(p.te)
                p.s_vec.wait(p.te)
                p.te.matmul(out=ps_sc[pb][:], lhsT=kb[:, m * P:(m + 1) * P],
                            rhs=qb[:, cq * 512:(cq + 1) * 512],
                            start=True, stop=False)
                p.s_mm.inc(p.te.matmul(out=ps_sc[pb][:], lhsT=c["ones_row_b"][:],
                                       rhs=negB[:, cq * 512:(cq + 1) * 512],
                                       start=False, stop=True))
                p.s_mm.wait(p.se)
                p.s_act.inc(p.se.activation(out=probsT[:, m * 512:(m + 1) * 512],
                                            in_=ps_sc[pb][:], func=AF.Exp,
                                            scale=SCALE_ATT))
                if m >= nsk - 4:
                    mi = m - (nsk - 4)
                    p.s_act.wait(p.ve)
                    p.s_vec.inc(p.ve.tensor_tensor(
                        out=probsT[:, m * 512:(m + 1) * 512],
                        in0=probsT[:, m * 512:(m + 1) * 512],
                        in1=c["masks_b"][:, mi * 512:(mi + 1) * 512], op=ALU.mult))
            p.bar()
            for m in range(nsk):
                p.te.matmul(out=ps_n[:, 0:512], lhsT=c["ones_col_b"][:],
                            rhs=probsT[:, m * 512:(m + 1) * 512],
                            start=(m == 0), stop=(m == nsk - 1))
            for m in range(nsk):
                ins = p.te.matmul(out=ps_z[:], lhsT=vT[:, m * HD:(m + 1) * HD],
                                  rhs=probsT[:, m * 512:(m + 1) * 512],
                                  start=(m == 0), stop=(m == nsk - 1))
            p.s_mm.inc(ins)
            p.s_mm.wait(p.ve)
            p.s_vec.inc(p.ve.tensor_scalar(out=rows[:, 3 * S:3 * S + 512],
                                           in0=ps_n[:, 0:512],
                                           scalar1=1e-30, scalar2=None, op0=ALU.add))
            p.s_vec.wait(p.ve)
            p.s_vec.inc(p.ve.reciprocal(out=rows[:, 3 * S:3 * S + 512],
                                        in_=rows[:, 3 * S:3 * S + 512]))
            p.s_vec.wait(p.te)
            p.s_mm.inc(p.te.matmul(out=ps_r[:], lhsT=c["ones_row_f"][:],
                                   rhs=rows[:, 3 * S:3 * S + 512],
                                   start=True, stop=True))
            p.s_mm.wait(p.se)
            p.s_act.inc(p.se.activation(out=zsb[:], in_=ps_z[:], func=AF.Copy))
            p.s_act.wait(p.ve)
            p.s_vec.inc(p.ve.tensor_tensor(out=zsb[:], in0=zsb[:], in1=ps_r[:],
                                           op=ALU.mult))
            p.s_vec.wait(p.sy)
            p.s_sto.dma(p.sy.dma_start(out=z_dram[h, :, cq * 512:(cq + 1) * 512],
                                       in_=zsb[:]))
            p.s_sto.wait(p.se)
            p.s_sto.wait(p.ve)
            p.s_sto.wait(p.te)
            p.bar()
    p.pop()


def emit_qkv_head_f32(p, qkv_ap, h, y32_dram, wbuf32, yrow, ps, evac):
    p.s_ldx.dma(p.sy.dma_start(out=wbuf32[:], in_=qkv_ap[h]))
    p.s_ldx.wait(p.te)
    mm_after_tt = []
    for tt in range(NT):
        b = tt % 2
        if tt >= 2:
            p.se.wait_ge(p.s_mm.h, mm_after_tt[tt - 2])
        for ch in range(NC):
            p.s_lds.dma(p.se.dma_start(
                out=yrow[b][:, ch * 512:(ch + 1) * 512],
                in_=y32_dram[ch, :, tt * 512:(tt + 1) * 512]))
        p.s_lds.wait(p.te)
        for mi in range(3):
            for ch in range(NC):
                ins = p.te.matmul(
                    out=ps[:],
                    lhsT=wbuf32[:, (mi * NC + ch) * HD:(mi * NC + ch + 1) * HD],
                    rhs=yrow[b][:, ch * 512:(ch + 1) * 512],
                    start=(ch == 0), stop=(ch == NC - 1))
            p.s_mm.inc(ins)
            evac(mi, tt, ps)
        mm_after_tt.append(p.s_mm.n)



def emit_att0_heads(p, c, g, y32_dram, att_dram):
    p.push()
    wbuf32 = p.sb([P, 3 * NC * HD], F32)
    yrow = [p.sb([P, NC * 512], F32) for _ in range(2)]
    ps = p.psum([P, 512])
    qb = p.sb([P, S], BF16)
    kb = p.sb([P, S], BF16)
    vb = p.sb([P, S], BF16)
    kT = p.sb([P, NC * HD], BF16)
    vT = p.sb([P, NC * HD], BF16)
    ps_t = p.psum([P, P], BF16)
    ps_st = p.psum([P, 64])
    ps_kv = p.psum([P, P])
    ps_o = p.psum([P, 64])
    st_bf = p.sb([P, 64], BF16)
    Pst = p.sb([P, P], F32)
    Pbf = p.sb([P, P], BF16)
    ob = p.sb([P, 64], BF16)
    tmp = p.sb([P, 512], F32)

    for h in range(H):
        def evac(mi, tt, ps_):
            dst = (qb, kb, vb)[mi]
            sl_ = dst[:, tt * 512:(tt + 1) * 512]
            p.s_mm.wait(p.ve)
            if mi == 0:
                p.s_vec.inc(p.ve.tensor_scalar(out=sl_, in0=ps_[:],
                                               scalar1=SCALE_ATT, scalar2=None,
                                               op0=ALU.mult))
            else:
                p.s_vec.inc(p.ve.tensor_scalar(out=tmp[:], in0=ps_[:], scalar1=0.0,
                                               scalar2=2.0, op0=ALU.is_ge,
                                               op1=ALU.mult))
                p.s_vec.wait(p.ve)
                p.s_vec.inc(p.ve.tensor_scalar(out=sl_, in0=tmp[:], scalar1=-1.0,
                                               scalar2=None, op0=ALU.add))
            p.s_vec.wait(p.te)

        emit_qkv_head_f32(p, g["a0_qkv"], h, y32_dram, wbuf32, yrow, ps, evac)
        p.bar()
        for sc_ in range(NC):
            for src, dst in ((kb, kT), (vb, vT)):
                p.s_mm.inc(p.te.transpose(out=ps_t[:],
                                          in_=src[:, sc_ * P:(sc_ + 1) * P],
                                          identity=c["ident_b"][:]))
                p.s_mm.wait(p.se)
                p.s_act.inc(p.se.activation(out=dst[:, sc_ * P:(sc_ + 1) * P],
                                            in_=ps_t[:], func=AF.Copy))
                p.s_act.wait(p.te)
        p.s_act.wait(p.ve)
        p.s_vec.inc(p.ve.memset(Pst[:], 0.0))
        p.s_vec.inc(p.ve.memset(Pbf[:], 0.0))
        p.s_vec.wait(p.te)
        p.bar()
        for n in range(32):
            sc2 = n // 2
            base = (n % 2) * 64
            s0 = n * 64
            p.s_mm.inc(p.te.matmul(out=ps_st[base:base + 64, :],
                                   lhsT=kb[:, s0:s0 + 64],
                                   rhs=qb[:, s0:s0 + 64], start=True, stop=True))
            p.s_mm.wait(p.ve)
            p.s_vec.inc(p.ve.tensor_tensor(out=st_bf[base:base + 64, :],
                                           in0=ps_st[base:base + 64, :],
                                           in1=c["triu64_b"][base:base + 64, :],
                                           op=ALU.mult))
            p.s_vec.wait(p.te)
            p.s_mm.inc(p.te.matmul(
                out=ps_kv[:], lhsT=kT[base:base + 64, sc2 * HD:(sc2 + 1) * HD],
                rhs=vT[base:base + 64, sc2 * HD:(sc2 + 1) * HD],
                start=True, stop=True))
            p.te.matmul(out=ps_o[:],
                        lhsT=vT[base:base + 64, sc2 * HD:(sc2 + 1) * HD],
                        rhs=st_bf[base:base + 64, :], start=True, stop=False)
            p.s_mm.inc(p.te.matmul(out=ps_o[:], lhsT=Pbf[:],
                                   rhs=qb[:, s0:s0 + 64], start=False, stop=True))
            p.s_mm.wait(p.ve)
            p.s_mm.wait(p.se)
            p.s_vec.inc(p.ve.tensor_tensor(out=Pst[:], in0=Pst[:], in1=ps_kv[:],
                                           op=ALU.add))
            p.s_vec.wait(p.ve)
            p.s_vec.inc(p.ve.tensor_copy(out=Pbf[:], in_=Pst[:]))
            p.s_vec.wait(p.te)
            p.s_act.inc(p.se.activation(out=ob[:], in_=ps_o[:], func=AF.Copy))
            p.s_act.wait(p.sy)
            p.s_sto.dma(p.sy.dma_start(out=att_dram[h, :, s0:s0 + 64], in_=ob[:]))
            p.s_sto.wait(p.se)
            p.s_sto.wait(p.te)
        p.bar()
    p.pop()



def emit_ffn1(p, c, g, scr, wc, x_in, x_out):
    root = p.push()
    mhA = p.sb([R, S], BF16)
    b_sb = p.sb([R, F1], BF16)

    p.push()
    y = p.sb([P, NC * S], BF16)
    rstd_r = p.sb([1, S], F32)
    emit_ln(p, c, x_in, y, 0, S, rstd_r=rstd_r)
    p.s_sto.dma(p.sy.dma_start(out=scr["scr_r"], in_=rstd_r[:]))
    p.push()
    ps_rb = p.psum([P, 512])
    w1c = p.sb([P, NC], F32)
    p.s_ldx.dma(p.sy.dma_start(out=w1c[:], in_=g["f1_ln1"]))
    p.bar()
    p.s_ldx.wait(p.ve)
    for tt in range(NT):
        p.s_vec.wait(p.te)
        p.s_mm.inc(p.te.matmul(out=ps_rb[:], lhsT=c["ones_row_f"][:],
                               rhs=rstd_r[:, tt * 512:(tt + 1) * 512],
                               start=True, stop=True))
        p.s_mm.wait(p.ve)
        p.s_mm.wait(p.sy)
        for ch in range(NC):
            ysl = y[:, ch * S + tt * 512: ch * S + tt * 512 + 512]
            p.s_sto.dma(p.sy.dma_start(
                out=scr["scr_y"][ch, :, tt * 512:(tt + 1) * 512], in_=ysl))
            p.s_sto.wait(p.ve)
            p.s_vec.inc(p.ve.scalar_tensor_tensor(
                out=ysl, in0=ysl, scalar=w1c[:, ch:ch + 1], in1=ps_rb[:],
                op0=ALU.mult, op1=ALU.mult))
        p.s_vec.wait(p.sy)
        p.bar()
    p.pop()
    emit_att_softmax_heads(p, c, wc["qkv_f1"], y, scr["scr_z"])
    p.pop()
    p.bar()
    emit_oproj_residual(p, c, wc["wo_f1"], scr["scr_z"], None, None,
                        out_att=scr["scr_att"])
    p.bar()

    p.push()
    a_sb = p.sb([P, NC * R], BF16)
    ps_r16 = p.psum([R, 2 * 512])
    ps_rb = p.psum([P, 512])
    rowf = p.sb([R, 2 * 512], F32)
    rstd_r2 = p.sb([1, S], F32)
    w2c = p.sb([P, NC], F32)
    ytile = p.sb([P, 512], BF16)
    y2t = p.sb([P, 512], BF16)
    atile = p.sb([P, 512], BF16)
    p.s_ldx.dma(p.sy.dma_start(out=rstd_r2[:], in_=scr["scr_r"]))
    p.s_ldx.dma(p.sy.dma_start(out=w2c[:], in_=g["f1_ln2"]))
    p.s_ldx.dma(p.sy.dma_start(out=a_sb[:], in_=wc["a_b"]))
    p.s_ldx.dma(p.sy.dma_start(out=b_sb[:], in_=wc["b_b"]))
    p.bar()
    p.s_ldx.wait(p.ve)
    p.s_ldx.wait(p.te)
    for tt in range(NT):
        p.s_mm.inc(p.te.matmul(out=ps_rb[:], lhsT=c["ones_row_f"][:],
                               rhs=rstd_r2[:, tt * 512:(tt + 1) * 512],
                               start=True, stop=True))
        p.s_mm.wait(p.ve)
        for ch in range(NC):
            p.s_ldx.dma(p.sy.dma_start(
                out=ytile[:], in_=scr["scr_y"][ch, :, tt * 512:(tt + 1) * 512]))
            p.s_ldx.dma(p.sy.dma_start(
                out=atile[:], in_=scr["scr_att"][ch, :, tt * 512:(tt + 1) * 512]))
            p.s_ldx.wait(p.ve)
            p.s_vec.inc(p.ve.scalar_tensor_tensor(
                out=y2t[:], in0=ytile[:], scalar=w2c[:, ch:ch + 1], in1=ps_rb[:],
                op0=ALU.mult, op1=ALU.mult))
            p.s_vec.wait(p.te)
            p.te.matmul(out=ps_r16[:, 0:512], lhsT=a_sb[:, ch * R:(ch + 1) * R],
                        rhs=y2t[:], start=(ch == 0), stop=(ch == NC - 1))
            ins = p.te.matmul(out=ps_r16[:, 512:1024],
                              lhsT=a_sb[:, ch * R:(ch + 1) * R],
                              rhs=atile[:], start=(ch == 0), stop=(ch == NC - 1))
            p.s_mm.inc(ins)
            p.s_mm.wait(p.sy)
        p.s_mm.wait(p.ve)
        p.s_mm.wait(p.se)
        p.s_act.inc(p.se.activation(out=rowf[:, 512:1024], in_=ps_r16[:, 512:1024],
                                    func=AF.Tanh))
        p.s_act.wait(p.ve)
        p.s_vec.inc(p.ve.tensor_copy(out=rowf[:, 0:512], in_=ps_r16[:, 0:512]))
        p.s_vec.wait(p.ve)
        p.s_vec.inc(p.ve.scalar_tensor_tensor(
            out=mhA[:, tt * 512:(tt + 1) * 512], in0=rowf[:, 512:1024], scalar=0.1,
            in1=rowf[:, 0:512], op0=ALU.mult, op1=ALU.mult))
        p.s_vec.wait(p.te)
        p.bar()
    p.pop()
    p.bar()

    TB = 512
    MSLAB = 16
    NSLAB = F1 // (MSLAB * P)
    for t0 in range(0, S, TB):
        p.push()
        tt = t0 // TB
        y2 = p.sb([P, NC * TB], BF16)
        acc = p.sb([P, NC * TB], F32)
        inter = p.sb([P, MSLAB * TB], BF16)
        wgu_b = [p.sb([P, 2 * NC * P], BF16) for _ in range(2)]
        wd_slab = p.sb([P, MSLAB * D], BF16)
        ps_g = p.psum([P, TB])
        ps_u = p.psum([P, TB])
        ps_d = p.psum([P, TB])
        ps_rb = p.psum([P, 512])
        gg = p.sb([P, TB], BF16)
        rstd_r2 = p.sb([1, S], F32)
        w2c = p.sb([P, NC], F32)
        p.s_ldx.dma(p.sy.dma_start(out=rstd_r2[:], in_=scr["scr_r"]))
        p.s_ldx.dma(p.sy.dma_start(out=w2c[:], in_=g["f1_ln2"]))
        p.bar()
        p.s_ldx.wait(p.te)
        p.s_ldx.wait(p.ve)
        p.s_mm.inc(p.te.matmul(out=ps_rb[:], lhsT=c["ones_row_f"][:],
                               rhs=rstd_r2[:, t0:t0 + TB], start=True, stop=True))
        p.s_mm.wait(p.ve)
        ytile = p.sb([P, 512], BF16)
        for ch in range(NC):
            p.s_ldx.dma(p.sy.dma_start(
                out=ytile[:], in_=scr["scr_y"][ch, :, t0:t0 + TB]))
            p.s_ldx.wait(p.ve)
            p.s_vec.inc(p.ve.scalar_tensor_tensor(
                out=y2[:, ch * TB:(ch + 1) * TB], in0=ytile[:],
                scalar=w2c[:, ch:ch + 1], in1=ps_rb[:], op0=ALU.mult, op1=ALU.mult))
            p.s_vec.wait(p.sy)
        p.bar()
        for sl_i in range(NSLAB):
            for jj in range(MSLAB):
                j = sl_i * MSLAB + jj
                p.s_lds.dma(p.se.dma_start(out=wd_slab[:, jj * D:(jj + 1) * D],
                                           in_=wc["wdn"][j]))
            mm_after = []
            for jj in range(MSLAB):
                j = sl_i * MSLAB + jj
                b = jj % 2
                if jj >= 2:
                    p.sy.wait_ge(p.s_mm.h, mm_after[jj - 2])
                p.s_ldx.dma(p.sy.dma_start(out=wgu_b[b][:, :NC * P],
                                           in_=wc["win"][j]))
                p.s_ldx.dma(p.sy.dma_start(out=wgu_b[b][:, NC * P:],
                                           in_=wc["wup"][j]))
                p.s_ldx.wait(p.te)
                p.s_vec.wait(p.te)
                for ch in range(NC):
                    p.te.matmul(out=ps_g[:], lhsT=wgu_b[b][:, ch * P:(ch + 1) * P],
                                rhs=y2[:, ch * TB:(ch + 1) * TB],
                                start=(ch == 0), stop=False)
                p.te.matmul(out=ps_g[:], lhsT=b_sb[:, j * P:(j + 1) * P],
                            rhs=mhA[:, t0:t0 + TB], start=False, stop=True)
                for ch in range(NC):
                    ins = p.te.matmul(
                        out=ps_u[:],
                        lhsT=wgu_b[b][:, NC * P + ch * P: NC * P + (ch + 1) * P],
                        rhs=y2[:, ch * TB:(ch + 1) * TB],
                        start=(ch == 0), stop=(ch == NC - 1))
                p.s_mm.inc(ins)
                mm_after.append(p.s_mm.n)
                p.s_mm.wait(p.se)
                p.s_act.inc(p.se.activation(out=gg[:], in_=ps_g[:], func=AF.Silu))
                p.s_act.wait(p.ve)
                p.s_vec.inc(p.ve.tensor_tensor(out=inter[:, jj * TB:(jj + 1) * TB],
                                               in0=gg[:], in1=ps_u[:], op=ALU.mult))
            p.bar()
            p.s_lds.wait(p.te)
            p.s_ldg.wait(p.te)
            for hc in range(NC):
                for jj in range(MSLAB):
                    ins = p.te.matmul(
                        out=ps_d[:],
                        lhsT=wd_slab[:, jj * D + hc * P: jj * D + (hc + 1) * P],
                        rhs=inter[:, jj * TB:(jj + 1) * TB],
                        start=(jj == 0), stop=(jj == MSLAB - 1))
                p.s_mm.inc(ins)
                if sl_i == 0:
                    p.s_mm.wait(p.se)
                    p.s_act.inc(p.se.activation(out=acc[:, hc * TB:(hc + 1) * TB],
                                                in_=ps_d[:], func=AF.Copy))
                    p.s_act.wait(p.te)
                else:
                    p.s_mm.wait(p.ve)
                    p.s_vec.inc(p.ve.tensor_tensor(
                        out=acc[:, hc * TB:(hc + 1) * TB],
                        in0=acc[:, hc * TB:(hc + 1) * TB], in1=ps_d[:], op=ALU.add))
                    p.s_vec.wait(p.te)
            p.bar()
        xb = p.sb([P, 512], F32)
        yb2 = p.sb([P, 512], BF16)
        xo = p.sb([P, 512], F32)
        for ch in range(NC):
            p.s_ldx.dma(p.sy.dma_start(out=xb[:], in_=x_in[ch, :, t0:t0 + TB]))
            p.s_ldx.dma(p.sy.dma_start(out=yb2[:],
                                       in_=scr["scr_y"][ch, :, t0:t0 + TB]))
            p.s_ldx.wait(p.ve)
            p.s_vec.inc(p.ve.scalar_tensor_tensor(
                out=xo[:], in0=acc[:, ch * TB:(ch + 1) * TB], scalar=0.5,
                in1=xb[:], op0=ALU.mult, op1=ALU.add))
            p.s_vec.wait(p.ve)
            p.s_vec.inc(p.ve.scalar_tensor_tensor(out=xo[:], in0=yb2[:], scalar=0.5,
                                                  in1=xo[:], op0=ALU.mult,
                                                  op1=ALU.add))
            p.s_vec.wait(p.sy)
            p.s_sto.dma(p.sy.dma_start(out=x_out[ch, :, t0:t0 + TB], in_=xo[:]))
            p.s_sto.wait(p.ve)
            p.bar()
        p.pop()
        p.bar()
    p.pop()



W_DTYPES = {"gup_packedT": I32, "down_packedT": I32}


def declare_dram(nc, w, names):
    g = {}
    for n in names:
        arr = w[n]
        dt = W_DTYPES.get(n, F32)
        g[n] = nc.dram_tensor(n, list(arr.shape), dt, kind="ExternalInput").ap()
    return g


CONST_NAMES = ["ones_col", "ones_row", "ident", "diag_masks", "triu64"]
FFN0_NAMES = ["gup_packedT", "gup_scalesT", "down_packedT", "down_scalesT"]
ATT0_NAMES = ["a0_qkv", "a0_wo"]
ATT1_NAMES = ["a1_qkv", "a1_wo"]
FFN1_NAMES = ["f1_qkv", "f1_wo", "f1_a", "f1_b", "f1_win",
              "f1_wup", "f1_wdn", "f1_ln1", "f1_ln2"]
ALL_NAMES = CONST_NAMES + FFN0_NAMES + ATT0_NAMES + ATT1_NAMES + FFN1_NAMES


def make_ffn0_cache(nc):
    return {
        "wg": nc.dram_tensor("cache_wg", [M0 // P, P, NC * P], BF16).ap(),
        "wu": nc.dram_tensor("cache_wu", [M0 // P, P, NC * P], BF16).ap(),
        "wd": nc.dram_tensor("cache_wd", [128, P, D], BF16).ap(),
    }


def make_scr(nc):
    return {
        "scr_y": nc.dram_tensor("scr_y", [NC, P, S], BF16).ap(),
        "scr_z": nc.dram_tensor("scr_z", [NC, P, S], BF16).ap(),
        "scr_att": nc.dram_tensor("scr_att", [NC, P, S], BF16).ap(),
        "scr_r": nc.dram_tensor("scr_r", [1, S], F32).ap(),
        "scr_y32": nc.dram_tensor("scr_y32", [NC, P, S], F32).ap(),
    }


def make_wc(nc):
    return {
        "qkv_a1": nc.dram_tensor("wc_qkv_a1", [H, P, 3 * NC * HD], BF16).ap(),
        "qkv_f1": nc.dram_tensor("wc_qkv_f1", [H, P, 3 * NC * HD], BF16).ap(),
        "wo_a0": nc.dram_tensor("wc_wo_a0", [NC, P, D], BF16).ap(),
        "wo_a1": nc.dram_tensor("wc_wo_a1", [NC, P, D], BF16).ap(),
        "wo_f1": nc.dram_tensor("wc_wo_f1", [NC, P, D], BF16).ap(),
        "win": nc.dram_tensor("wc_win", [F1 // P, P, NC * P], BF16).ap(),
        "wup": nc.dram_tensor("wc_wup", [F1 // P, P, NC * P], BF16).ap(),
        "wdn": nc.dram_tensor("wc_wdn", [64, P, D], BF16).ap(),
        "a_b": nc.dram_tensor("wc_a_b", [P, NC * R], BF16).ap(),
        "b_b": nc.dram_tensor("wc_b_b", [R, F1], BF16).ap(),
    }


def finish(p):
    p.bar()
    p.sy.wait_ge(p.s_sto.h, p.s_sto.n)
    p.se.wait_ge(p.s_sto2.h, p.s_sto2.n)
    p.bar()


def _new_nc():
    return bass.Bass("TRN2", target_bir_lowering=False)


def build_fused(w):
    nc = _new_nc()
    g = declare_dram(nc, w, ALL_NAMES)
    x_in = nc.dram_tensor("x_in", [NC, P, S], F32, kind="ExternalInput").ap()
    x_out = nc.dram_tensor("x_out", [NC, P, S], F32, kind="ExternalOutput").ap()
    xa = nc.dram_tensor("x_a", [NC, P, S], F32).ap()
    xb = nc.dram_tensor("x_b", [NC, P, S], F32).ap()
    cache = make_ffn0_cache(nc)
    scr = make_scr(nc)
    wc = make_wc(nc)
    with ExitStack() as ctx:
        p = Prog(nc, ctx)
        c = emit_const_load(p, g)
        emit_weight_prep(p, g, wc)
        emit_ffn0_dequant(p, g, cache)
        emit_ffn0_compute(p, c, g, cache, x_in, xa)
        p.push()
        emit_ln(p, c, xa, None, 0, S, y32_dram=scr["scr_y32"])
        p.bar()
        emit_att0_heads(p, c, g, scr["scr_y32"], scr["scr_z"])
        p.pop()
        p.bar()
        emit_oproj_residual(p, c, wc["wo_a0"], scr["scr_z"], xa, xb)
        p.bar()
        emit_ffn0_compute(p, c, g, cache, xb, xa)
        emit_ffn1(p, c, g, scr, wc, xa, xb)
        p.push()
        y = p.sb([P, NC * S], BF16)
        emit_ln(p, c, xb, y, 0, S)
        emit_att_softmax_heads(p, c, wc["qkv_a1"], y, scr["scr_z"])
        p.pop()
        p.bar()
        emit_oproj_residual(p, c, wc["wo_a1"], scr["scr_z"], xb, xa)
        p.bar()
        emit_ffn1(p, c, g, scr, wc, xa, x_out)
        finish(p)
    return nc


def kernel(**inputs):
    w = prep_weights(inputs)
    xs = [prep_x(inputs["L0_attn_x"], b) for b in range(8)]
    nc = build_fused(w)
    in_maps = []
    for b in range(8):
        m = {n: w[n] for n in ALL_NAMES}
        m["x_in"] = xs[b]
        in_maps.append(m)
    res = run_bass_kernel_spmd(nc, in_maps, core_ids=list(range(8)))
    return np.stack([post_out(np.asarray(res.results[b]["x_out"]))
                     for b in range(8)]).astype(np.float32)


# version 05
# speedup vs baseline: 1.1117x; 1.1117x over previous
import numpy as np
from contextlib import ExitStack

import concourse.bass as bass
import concourse.mybir as mybir
from concourse.bass_utils import run_bass_kernel_spmd

F32 = mybir.dt.float32
BF16 = mybir.dt.bfloat16
I32 = mybir.dt.int32
F16 = mybir.dt.float16
AF = mybir.ActivationFunctionType
ALU = mybir.AluOpType
AX = mybir.AxisListType

P = 128
D = 2048
S = 2048
NC = D // P
H = 16
HD = 128
M0 = 16384
F1 = 8192
R = 16
SCALE_ATT = float(1.0 / np.sqrt(128.0))
NT = S // 512

PI = np.concatenate([np.arange(0, D, 2), np.arange(1, D, 2)])
SIG = np.concatenate([np.arange(0, M0, 2), np.arange(1, M0, 2)])


def _ct(a):
    return np.ascontiguousarray(a)



def prep_weights(ins):
    w = {}
    pi = PI
    gp = np.asarray(ins["L0_ffn_gate_W_packed"])
    up = np.asarray(ins["L0_ffn_up_W_packed"])
    w["gup_packedT"] = _ct(np.stack([gp[SIG].T, up[SIG].T]).reshape(2, 8, P, M0))
    gs = np.asarray(ins["L0_ffn_gate_scales"])
    us = np.asarray(ins["L0_ffn_up_scales"])
    w["gup_scalesT"] = _ct(np.stack([gs[SIG].T, us[SIG].T]))
    dp = np.asarray(ins["L0_ffn_down_W_packed"])
    w["down_packedT"] = _ct(dp[pi].T.reshape(64, P, D))
    ds = np.asarray(ins["L0_ffn_down_scales"])
    w["down_scalesT"] = _ct(ds[pi].T)

    def prepT(a):
        return _ct(np.asarray(a).T[pi].reshape(NC, P, np.asarray(a).shape[0]))

    def dd_head_tile0(a):
        return (np.asarray(a).T[pi].reshape(NC, P, H, HD)
                .transpose(2, 1, 0, 3).reshape(H, P, NC * HD))

    w["a0_qkv"] = _ct(np.concatenate(
        [dd_head_tile0(ins[f"L0_attn_W{s}"]) for s in "qkv"], axis=2))
    w["a0_wo"] = _ct(np.asarray(ins["L0_attn_Wo"])[pi, :].T.reshape(NC, P, D))

    def head_tile(a):
        return a.reshape(H, NC, P, HD).transpose(0, 2, 1, 3).reshape(H, P, NC * HD)

    wq1 = np.asarray(ins["L1_attn_W_Q"]).transpose(0, 2, 1)[:, pi, :]
    wk1 = np.asarray(ins["L1_attn_W_K"]).transpose(0, 2, 1)[:, pi, :]
    wv1 = np.asarray(ins["L1_attn_W_V"]).transpose(0, 2, 1)[:, pi, :]
    wo1 = np.asarray(ins["L1_attn_W_O"])
    w["a1_qkv"] = _ct(np.concatenate([head_tile(wq1), head_tile(wk1),
                                      head_tile(wv1)], axis=2))
    w["a1_wo"] = _ct(wo1[:, pi, :].transpose(0, 2, 1))

    def dd_head_tile(a):
        return (np.asarray(a).T[pi].reshape(NC, P, H, HD)
                .transpose(2, 1, 0, 3).reshape(H, P, NC * HD))
    w["f1_qkv"] = _ct(np.concatenate(
        [dd_head_tile(ins[f"L1_ffn_W{s}"]) for s in "qkv"], axis=2))
    w["f1_wo"] = _ct(np.asarray(ins["L1_ffn_Wo"])[pi, :].T.reshape(NC, P, D))
    w["f1_a"] = _ct(np.asarray(ins["L1_ffn_A"])[pi].reshape(NC, P, R)
                    .transpose(1, 0, 2).reshape(P, NC * R))
    w["f1_b"] = _ct(np.asarray(ins["L1_ffn_B"]))
    def jtile(a):
        return _ct(np.asarray(a).T[pi].reshape(NC, P, F1 // P, P)
                   .transpose(2, 1, 0, 3).reshape(F1 // P, P, NC * P))
    w["f1_win"] = jtile(ins["L1_ffn_W_in_base"])
    w["f1_wup"] = jtile(ins["L1_ffn_W_up_base"])
    w["f1_wdn"] = _ct(np.asarray(ins["L1_ffn_W_down"])[pi, :].T.reshape(64, P, D))
    w["f1_ln1"] = _ct(np.asarray(ins["L1_ffn_ln1_w"])[pi].reshape(NC, P).T)
    w["f1_ln2"] = _ct(np.asarray(ins["L1_ffn_ln2_w"])[pi].reshape(NC, P).T)

    w["ones_col"] = np.ones((P, 1), np.float32)
    w["ones_row"] = np.ones((1, P), np.float32)
    w["ident"] = np.eye(P, dtype=np.float32)
    masks = np.zeros((4, P, 512), np.float32)
    for m in range(4):
        masks[m] = (m * 128 + np.arange(P)[:, None]) <= np.arange(512)[None, :]
    w["diag_masks"] = masks
    w["triu64"] = _ct(np.tile(np.triu(np.ones((64, 64), np.float32)), (2, 1)))
    return w


def prep_x(x_full, core):
    return _ct(np.asarray(x_full)[core].T[PI].reshape(NC, P, S))


def post_out(x_out):
    inv = np.argsort(PI)
    return _ct(np.asarray(x_out, np.float32).reshape(D, S)[inv].T)



class Sem:
    def __init__(self, ctx, nc, name):
        self.h = ctx.enter_context(nc.semaphore(name))
        self.n = 0

    def inc(self, instr, k=1):
        instr.then_inc(self.h, k)
        self.n += k

    def dma(self, instr):
        self.inc(instr, 16)

    def wait(self, eng, n=None):
        eng.wait_ge(self.h, self.n if n is None else n)


class Prog:
    def __init__(self, nc, ctx):
        self.nc = nc
        self.ctx = ctx
        self._stack = [ctx]
        self.te = nc.tensor
        self.ve = nc.vector
        self.se = nc.scalar
        self.po = nc.gpsimd
        self.sy = nc.sync
        for name in ["ldx", "lds", "ldg", "mm", "act", "vec", "pool", "sto", "sto2"]:
            setattr(self, "s_" + name, Sem(ctx, nc, name))
        self.uid = 0

    def push(self):
        st = ExitStack()
        self._stack.append(st)
        return st

    def pop(self):
        self._stack.pop().close()

    def _c(self):
        return self._stack[-1]

    def name(self, base):
        self.uid += 1
        return f"{base}{self.uid}"

    def sb(self, shape, dt):
        return self._c().enter_context(self.nc.sbuf_tensor(self.name("t"), shape, dt))

    def psum(self, shape, dt=F32):
        return self._c().enter_context(self.nc.psum_tensor(self.name("q"), shape, dt))

    def bar(self):
        self.nc.all_engine_barrier()


def emit_const_load(p, g):
    c = {}
    c["ones_col_f"] = p.sb([P, 1], F32)
    c["ones_row_f"] = p.sb([1, P], F32)
    c["ones_row_b"] = p.sb([1, P], BF16)
    c["ones_col_b"] = p.sb([P, 1], BF16)
    c["ident_b"] = p.sb([P, P], BF16)
    c["masks_b"] = p.sb([P, 4 * 512], BF16)
    c["triu64_b"] = p.sb([P, 64], BF16)
    p.s_ldx.dma(p.sy.dma_start(out=c["ones_col_f"][:], in_=g["ones_col"]))
    p.s_ldx.dma(p.sy.dma_start(out=c["ones_row_f"][:], in_=g["ones_row"]))
    p.s_ldg.dma(p.po.dma_start(out=c["ones_col_b"][:], in_=g["ones_col"]))
    p.s_ldg.dma(p.po.dma_start(out=c["ones_row_b"][:], in_=g["ones_row"]))
    p.s_ldg.dma(p.po.dma_start(out=c["ident_b"][:], in_=g["ident"]))
    p.s_ldg.dma(p.po.dma_start(
        out=c["masks_b"][:].rearrange("p (m f) -> p m f", m=4),
        in_=g["diag_masks"].transpose([1, 0, 2])))
    p.s_ldg.dma(p.po.dma_start(out=c["triu64_b"][:], in_=g["triu64"]))
    p.bar()
    p.s_ldx.wait(p.ve)
    p.s_ldg.wait(p.ve)
    p.bar()
    return c


def emit_ln(p, c, x_dram, y_sb, t0, tw, rstd_r=None, x_keep=None, y32_dram=None):
    p.push()
    xb = x_keep if x_keep is not None else p.sb([P, NC * 512], F32)
    sqb = p.sb([P, 2 * 512], F32)
    ps_stat = p.psum([1, 2 * 512])
    ps_bc = p.psum([P, 2 * 512])
    rowbuf = p.sb([1, 6 * 512], F32)
    tmp = p.sb([P, 512], F32)
    tmp32 = p.sb([P, 2 * 512], F32) if y32_dram is not None else None
    sto_after = []
    for st in range(tw // 512):
        tt = t0 + st * 512

        def xt(ch):
            if x_keep is not None:
                return xb[:, ch * tw + st * 512: ch * tw + st * 512 + 512]
            return xb[:, ch * 512:(ch + 1) * 512]

        for ch in range(NC):
            p.s_ldx.dma(p.sy.dma_start(out=xt(ch), in_=x_dram[ch, :, tt:tt + 512]))
        p.bar()
        p.s_ldx.wait(p.te)
        p.s_ldx.wait(p.se)
        mm_after = []
        for ch in range(NC):
            sq = sqb[:, (ch % 2) * 512:(ch % 2) * 512 + 512]
            p.te.matmul(out=ps_stat[:, 0:512], lhsT=c["ones_col_f"][:], rhs=xt(ch),
                        start=(ch == 0), stop=(ch == NC - 1))
            if ch >= 2:
                p.s_mm.wait(p.se, mm_after[ch - 2])
            p.s_act.inc(p.se.activation(out=sq, in_=xt(ch), func=AF.Square))
            p.s_act.wait(p.te)
            ins = p.te.matmul(out=ps_stat[:, 512:1024], lhsT=c["ones_col_f"][:],
                              rhs=sq, start=(ch == 0), stop=(ch == NC - 1))
            p.s_mm.inc(ins)
            mm_after.append(p.s_mm.n)
        p.s_mm.wait(p.ve)
        mean = rowbuf[:, 0:512]
        var = rowbuf[:, 512:1024]
        t2 = rowbuf[:, 1024:1536]
        rstd = rowbuf[:, 1536:2048]
        cc = rowbuf[:, 2048:2560]
        t5 = rowbuf[:, 2560:3072]
        p.s_vec.inc(p.ve.tensor_scalar(out=mean, in0=ps_stat[:, 0:512],
                                       scalar1=1.0 / D, scalar2=None, op0=ALU.mult))
        p.s_vec.inc(p.ve.tensor_scalar(out=var, in0=ps_stat[:, 512:1024],
                                       scalar1=1.0 / D, scalar2=None, op0=ALU.mult))
        p.s_vec.wait(p.ve)
        p.s_vec.inc(p.ve.tensor_tensor(out=t2, in0=mean, in1=mean, op=ALU.mult))
        p.s_vec.wait(p.ve)
        p.s_vec.inc(p.ve.tensor_tensor(out=var, in0=var, in1=t2, op=ALU.subtract))
        p.s_vec.wait(p.ve)
        p.s_vec.inc(p.ve.tensor_scalar(out=t2, in0=var, scalar1=1e-5,
                                       scalar2=None, op0=ALU.add))
        p.s_vec.wait(p.se)
        p.s_act.inc(p.se.activation(out=t2, in_=t2, func=AF.Sqrt))
        p.s_act.wait(p.ve)
        p.s_vec.inc(p.ve.reciprocal(out=rstd, in_=t2))
        p.s_vec.wait(p.ve)
        p.s_vec.inc(p.ve.tensor_tensor(out=t2, in0=mean, in1=rstd, op=ALU.mult))
        p.s_vec.wait(p.ve)
        p.s_vec.inc(p.ve.tensor_scalar(out=cc, in0=t2, scalar1=-1.0,
                                       scalar2=None, op0=ALU.mult))
        if rstd_r is not None:
            rr = rstd_r[:, st * 512:(st + 1) * 512]
            p.s_vec.inc(p.ve.tensor_scalar(out=t5, in0=var, scalar1=1e-5,
                                           scalar2=None, op0=ALU.add))
            p.s_vec.wait(p.ve)
            p.s_vec.inc(p.ve.reciprocal(out=t5, in_=t5))
            p.s_vec.wait(p.ve)
            p.s_vec.inc(p.ve.tensor_tensor(out=t5, in0=var, in1=t5, op=ALU.mult))
            p.s_vec.wait(p.ve)
            p.s_vec.inc(p.ve.tensor_scalar(out=t5, in0=t5, scalar1=1e-6,
                                           scalar2=None, op0=ALU.add))
            p.s_vec.wait(p.se)
            p.s_act.inc(p.se.activation(out=t5, in_=t5, func=AF.Sqrt))
            p.s_act.wait(p.ve)
            p.s_vec.inc(p.ve.reciprocal(out=rr, in_=t5))
        p.s_vec.wait(p.te)
        p.te.matmul(out=ps_bc[:, 0:512], lhsT=c["ones_row_f"][:], rhs=rstd,
                    start=True, stop=True)
        p.s_mm.inc(p.te.matmul(out=ps_bc[:, 512:1024], lhsT=c["ones_row_f"][:],
                               rhs=cc, start=True, stop=True))
        p.s_mm.wait(p.ve)
        for ch in range(NC):
            p.s_vec.inc(p.ve.tensor_tensor(out=tmp[:], in0=xt(ch),
                                           in1=ps_bc[:, 0:512], op=ALU.mult))
            p.s_vec.wait(p.ve)
            if y32_dram is not None:
                y32t = tmp32[:, (ch % 2) * 512:(ch % 2) * 512 + 512]
                if len(sto_after) >= 2:
                    p.s_sto.wait(p.ve, sto_after[-2])
                p.s_vec.inc(p.ve.tensor_tensor(out=y32t, in0=tmp[:],
                                               in1=ps_bc[:, 512:1024], op=ALU.add))
                p.s_vec.wait(p.sy)
                p.s_sto.dma(p.sy.dma_start(
                    out=y32_dram[ch, :, tt:tt + 512], in_=y32t))
                sto_after.append(p.s_sto.n)
                if y_sb is not None:
                    ins = p.ve.tensor_scalar(
                        out=y_sb[:, ch * tw + st * 512: ch * tw + st * 512 + 512],
                        in0=y32t, scalar1=1.0, scalar2=None, op0=ALU.mult)
                    p.s_vec.inc(ins)
            else:
                ins = p.ve.tensor_tensor(
                    out=y_sb[:, ch * tw + st * 512: ch * tw + st * 512 + 512],
                    in0=tmp[:], in1=ps_bc[:, 512:1024], op=ALU.add)
                if ch == NC - 1:
                    p.s_vec.inc(ins)
        p.bar()
    p.pop()



def _dequant_tile(p, eng, vi, s_eff, out_lo, out_hi, scr):
    F_ = vi.shape[1]
    vh, h32, mag, sg2 = (scr[k] for k in ["vh", "h32", "mag", "sg2"])
    eng.tensor_scalar(out=h32[:, :F_], in0=vi, scalar1=7, scalar2=25,
                      op0=ALU.bitwise_and, op1=ALU.logical_shift_left)
    eng.tensor_tensor(out=mag[:, :F_], in0=h32[:, :F_].bitcast(F16)[:, 1::2],
                      in1=s_eff, op=ALU.mult)
    eng.tensor_scalar(out=sg2[:, :F_], in0=vi, scalar1=2, scalar2=2,
                      op0=ALU.logical_shift_right, op1=ALU.bitwise_and)
    eng.scalar_tensor_tensor(out=out_lo, in0=sg2[:, :F_], scalar=1.0,
                             in1=mag[:, :F_], op0=ALU.subtract, op1=ALU.mult)
    eng.tensor_scalar(out=vh[:, :F_], in0=vi, scalar1=4, scalar2=None,
                      op0=ALU.logical_shift_right)
    eng.tensor_scalar(out=h32[:, :F_], in0=vh[:, :F_], scalar1=7, scalar2=25,
                      op0=ALU.bitwise_and, op1=ALU.logical_shift_left)
    eng.tensor_tensor(out=mag[:, :F_], in0=h32[:, :F_].bitcast(F16)[:, 1::2],
                      in1=s_eff, op=ALU.mult)
    eng.tensor_scalar(out=sg2[:, :F_], in0=vh[:, :F_], scalar1=2, scalar2=2,
                      op0=ALU.logical_shift_right, op1=ALU.bitwise_and)
    return eng.scalar_tensor_tensor(out=out_hi, in0=sg2[:, :F_], scalar=1.0,
                                    in1=mag[:, :F_], op0=ALU.subtract, op1=ALU.mult)


def emit_ffn0_dequant(p, g, cache):
    st_ = p.push()
    MS = 2048
    lanes = []
    for li, (eng, qsem, ldq, ldsem, stq, stsem) in enumerate([
            (p.ve, p.s_vec, p.sy, p.s_ldx, p.sy, p.s_sto),
            (p.ve, p.s_pool, p.se, p.s_lds, p.se, p.s_sto2)]):
        scr = {k: p.sb([P, MS], I32) for k in ["vh", "h32", "sg2"]}
        scr["mag"] = p.sb([P, MS], F32)
        lanes.append(dict(
            eng=eng, qsem=qsem, ldq=ldq, ldsem=ldsem, stq=stq, stsem=stsem,
            scr=scr, vi=[p.sb([P, MS], I32) for _ in range(2)],
            sl=[p.sb([P, MS], F32) for _ in range(2)],
            ol=[p.sb([P, MS], BF16) for _ in range(2)],
            oh=[p.sb([P, MS], BF16) for _ in range(2)],
            i=0))

    jobs = []
    nj = MS // P
    for mat, wname in ((0, "wg"), (1, "wu")):
        for kc in range(8):
            for ms in range(0, M0, MS):
                j0 = ms // P
                dlo = cache[wname][j0:j0 + nj, :, kc * P:(kc + 1) * P]\
                    .transpose([1, 0, 2])
                dhi = cache[wname][j0:j0 + nj, :, (kc + 8) * P:(kc + 9) * P]\
                    .transpose([1, 0, 2])
                jobs.append((g["gup_packedT"][mat, kc, :, ms:ms + MS],
                             g["gup_scalesT"][mat, 8 * kc:8 * kc + 8, ms:ms + MS],
                             dlo, dhi))
    for mc in range(64):
        jobs.append((g["down_packedT"][mc], g["down_scalesT"][8 * mc:8 * mc + 8, :],
                     cache["wd"][mc], cache["wd"][mc + 64]))

    for gi, (src, srows, dlo, dhi) in enumerate(jobs):
        L = lanes[gi % 2]
        i, b = L["i"], L["i"] % 2
        eng = L["eng"]
        Fw = src.shape[-1]
        if i >= 2:
            L["ldq"].wait_ge(L["qsem"].h, i - 1)
        L["ldsem"].dma(L["ldq"].dma_start(out=L["vi"][b][:, :Fw], in_=src))
        L["ldsem"].dma(L["ldq"].dma_start(
            out=L["sl"][b][:, :Fw],
            in_=srows.unsqueeze(1).broadcast_to([8, 16, Fw])))
        L["ldsem"].wait(eng)
        if i >= 2:
            L["stsem"].wait(eng, 32 * (i - 1))
        eng.tensor_scalar(out=L["sl"][b][:, :Fw], in0=L["sl"][b][:, :Fw],
                          scalar1=-16384.0, scalar2=None, op0=ALU.mult)
        ins = _dequant_tile(p, eng, L["vi"][b][:, :Fw], L["sl"][b][:, :Fw],
                            L["ol"][b][:, :Fw], L["oh"][b][:, :Fw], L["scr"])
        L["qsem"].inc(ins)
        L["qsem"].wait(L["stq"])
        L["stsem"].dma(L["stq"].dma_start(out=dlo, in_=L["ol"][b][:, :Fw]))
        L["stsem"].dma(L["stq"].dma_start(out=dhi, in_=L["oh"][b][:, :Fw]))
        L["i"] += 1
    p.bar()
    p.pop()


def emit_ffn0_compute(p, c, g, cache, x_in, x_out):
    TB = 512
    MSLAB = 16
    NSLAB = M0 // (MSLAB * P)
    for t0 in range(0, S, TB):
        st_ = p.push()
        y = p.sb([P, NC * TB], BF16)
        emit_ln(p, c, x_in, y, t0, TB)
        acc = p.sb([P, NC * TB], F32)
        inter = p.sb([P, MSLAB * TB], BF16)
        wgu_b = [p.sb([P, 2 * NC * P], BF16) for _ in range(3)]
        wd_slab = p.sb([P, MSLAB * D], BF16)
        ps_g = [p.psum([P, TB]) for _ in range(2)]
        ps_u = [p.psum([P, TB]) for _ in range(2)]
        ps_d = [p.psum([P, TB]) for _ in range(2)]
        gg = p.sb([P, 2 * TB], BF16)
        for sl_i in range(NSLAB):
            mm_after = []
            vec_after = []
            for jj in range(MSLAB):
                j = sl_i * MSLAB + jj
                b = jj % 3
                pb = jj % 2
                if jj >= 3:
                    p.sy.wait_ge(p.s_mm.h, mm_after[jj - 3])
                    p.po.wait_ge(p.s_mm.h, mm_after[jj - 3])
                p.s_ldx.dma(p.sy.dma_start(out=wgu_b[b][:, :NC * P],
                                           in_=cache["wg"][j]))
                p.s_lds.dma(p.sy.dma_start(out=wd_slab[:, jj * D:(jj + 1) * D],
                                           in_=cache["wd"][j]))
                p.s_ldg.dma(p.po.dma_start(out=wgu_b[b][:, NC * P:],
                                           in_=cache["wu"][j]))
                p.s_ldx.wait(p.te)
                p.s_ldg.wait(p.te)
                if jj >= 2:
                    p.s_vec.wait(p.te, vec_after[jj - 2])
                for ch in range(NC):
                    p.te.matmul(out=ps_g[pb][:],
                                lhsT=wgu_b[b][:, ch * P:(ch + 1) * P],
                                rhs=y[:, ch * TB:(ch + 1) * TB],
                                start=(ch == 0), stop=(ch == NC - 1))
                for ch in range(NC):
                    ins = p.te.matmul(out=ps_u[pb][:],
                                      lhsT=wgu_b[b][:, NC * P + ch * P: NC * P + (ch + 1) * P],
                                      rhs=y[:, ch * TB:(ch + 1) * TB],
                                      start=(ch == 0), stop=(ch == NC - 1))
                p.s_mm.inc(ins)
                mm_after.append(p.s_mm.n)
                p.s_mm.wait(p.se)
                if jj >= 2:
                    p.s_vec.wait(p.se, vec_after[jj - 2])
                p.s_act.inc(p.se.activation(out=gg[:, pb * TB:(pb + 1) * TB],
                                            in_=ps_g[pb][:],
                                            func=AF.Gelu_apprx_tanh))
                p.s_act.wait(p.ve)
                p.s_vec.inc(p.ve.tensor_tensor(out=inter[:, jj * TB:(jj + 1) * TB],
                                               in0=gg[:, pb * TB:(pb + 1) * TB],
                                               in1=ps_u[pb][:], op=ALU.mult))
                vec_after.append(p.s_vec.n)
            p.bar()
            p.s_lds.wait(p.te)
            dn_after = []
            for hc in range(NC):
                pb = hc % 2
                if hc >= 2:
                    p.te.wait_ge((p.s_act if sl_i == 0 else p.s_vec).h,
                                 dn_after[hc - 2])
                for jj in range(MSLAB):
                    ins = p.te.matmul(
                        out=ps_d[pb][:],
                        lhsT=wd_slab[:, jj * D + hc * P: jj * D + (hc + 1) * P],
                        rhs=inter[:, jj * TB:(jj + 1) * TB],
                        start=(jj == 0), stop=(jj == MSLAB - 1))
                p.s_mm.inc(ins)
                if sl_i == 0:
                    p.s_mm.wait(p.se)
                    p.s_act.inc(p.se.activation(out=acc[:, hc * TB:(hc + 1) * TB],
                                                in_=ps_d[pb][:], func=AF.Copy))
                    dn_after.append(p.s_act.n)
                else:
                    p.s_mm.wait(p.ve)
                    p.s_vec.inc(p.ve.tensor_tensor(out=acc[:, hc * TB:(hc + 1) * TB],
                                                   in0=acc[:, hc * TB:(hc + 1) * TB],
                                                   in1=ps_d[pb][:], op=ALU.add))
                    dn_after.append(p.s_vec.n)
            p.bar()

        xb = p.sb([P, 512], F32)
        xo = p.sb([P, 512], F32)
        for ch in range(NC):
            p.s_ldx.dma(p.sy.dma_start(out=xb[:], in_=x_in[ch, :, t0:t0 + TB]))
            p.s_ldx.wait(p.ve)
            p.s_vec.inc(p.ve.scalar_tensor_tensor(out=xo[:], in0=acc[:, ch * TB:(ch + 1) * TB],
                                                  scalar=0.5, in1=xb[:],
                                                  op0=ALU.mult, op1=ALU.add))
            p.s_vec.wait(p.sy)
            p.s_sto.dma(p.sy.dma_start(out=x_out[ch, :, t0:t0 + TB], in_=xo[:]))
            p.s_sto.wait(p.ve)
            p.bar()
        p.pop()
        p.bar()



def emit_weight_prep(p, g, wc):
    p.push()
    stg = [p.sb([P, F1], F32) for _ in range(2)]
    ob = [p.sb([P, F1], BF16) for _ in range(2)]
    jobs = []
    for h in range(H):
        jobs.append((g["a1_qkv"][h], wc["qkv_a1"][h], P, 3 * NC * HD))
        jobs.append((g["f1_qkv"][h], wc["qkv_f1"][h], P, 3 * NC * HD))
    for tag, sn in (("wo_a0", "a0_wo"), ("wo_a1", "a1_wo"), ("wo_f1", "f1_wo")):
        for ci in range(NC):
            jobs.append((g[sn][ci], wc[tag][ci], P, D))
    for tag, sn in (("win", "f1_win"), ("wup", "f1_wup")):
        for j in range(F1 // P):
            jobs.append((g[sn][j], wc[tag][j], P, NC * P))
    for j in range(64):
        jobs.append((g["f1_wdn"][j], wc["wdn"][j], P, D))
    jobs.append((g["f1_a"], wc["a_b"][:], P, NC * R))
    jobs.append((g["f1_b"], wc["b_b"][:], R, F1))

    acts = []
    for i, (src, dst, np_, width) in enumerate(jobs):
        b = i % 2
        q = p.sy if b == 0 else p.se
        lsem = p.s_ldx if b == 0 else p.s_lds
        ssem = p.s_sto if b == 0 else p.s_sto2
        if i >= 2:
            q.wait_ge(p.s_act.h, acts[i - 2])
        lsem.dma(q.dma_start(out=stg[b][:np_, :width], in_=src))
        lsem.wait(p.se)
        p.s_act.inc(p.se.activation(out=ob[b][:np_, :width],
                                    in_=stg[b][:np_, :width], func=AF.Copy))
        acts.append(p.s_act.n)
        p.s_act.wait(q)
        ssem.dma(q.dma_start(out=dst, in_=ob[b][:np_, :width]))
    p.bar()
    p.pop()


def emit_qkv_head(p, qkv_cache, h, y, wbuf, ps, evac):
    p.s_ldx.dma(p.sy.dma_start(out=wbuf[:], in_=qkv_cache[h]))
    p.s_ldx.wait(p.te)
    vec_after = []
    for mi in range(3):
        for tt in range(NT):
            k = mi * NT + tt
            if k >= 2:
                p.s_vec.wait(p.te, vec_after[k - 2])
            for ch in range(NC):
                ins = p.te.matmul(
                    out=ps[k % 2][:],
                    lhsT=wbuf[:, (mi * NC + ch) * HD:(mi * NC + ch + 1) * HD],
                    rhs=y[:, ch * S + tt * 512: ch * S + tt * 512 + 512],
                    start=(ch == 0), stop=(ch == NC - 1))
            p.s_mm.inc(ins)
            evac(mi, tt, ps[k % 2])
            vec_after.append(p.s_vec.n)


def emit_oproj_residual(p, c, wo_cache, z_dram, x_in, x_out,
                        half=False, y_dram=None, out_att=None):
    p.push()
    zrow = p.sb([P, NC * 512], BF16)
    wo_all = p.sb([P, NC * D], BF16)
    ps_o = p.psum([P, 512])
    xb = p.sb([P, 512], F32)
    yb = p.sb([P, 512], BF16)
    ob = p.sb([P, 512], F32)
    obb = p.sb([P, 512], BF16)
    for ci in range(NC):
        p.s_ldx.dma(p.sy.dma_start(out=wo_all[:, ci * D:(ci + 1) * D],
                                   in_=wo_cache[ci]))
    p.bar()
    p.s_ldx.wait(p.te)
    for tt in range(NT):
        for ch in range(NC):
            p.s_ldx.dma(p.sy.dma_start(out=zrow[:, ch * 512:(ch + 1) * 512],
                                       in_=z_dram[ch, :, tt * 512:(tt + 1) * 512]))
        p.bar()
        p.s_ldx.wait(p.te)
        for co in range(NC):
            for ci in range(NC):
                ins = p.te.matmul(out=ps_o[:],
                                  lhsT=wo_all[:, ci * D + co * P: ci * D + (co + 1) * P],
                                  rhs=zrow[:, ci * 512:(ci + 1) * 512],
                                  start=(ci == 0), stop=(ci == NC - 1))
            p.s_mm.inc(ins)
            if out_att is not None:
                p.s_mm.wait(p.se)
                p.s_act.inc(p.se.activation(out=obb[:], in_=ps_o[:], func=AF.Copy))
                p.s_act.wait(p.sy)
                p.s_sto.dma(p.sy.dma_start(
                    out=out_att[co, :, tt * 512:(tt + 1) * 512], in_=obb[:]))
                p.s_sto.wait(p.se)
                p.s_sto.wait(p.te)
            else:
                p.s_ldx.dma(p.sy.dma_start(out=xb[:],
                                           in_=x_in[co, :, tt * 512:(tt + 1) * 512]))
                if y_dram is not None:
                    p.s_ldx.dma(p.sy.dma_start(
                        out=yb[:], in_=y_dram[co, :, tt * 512:(tt + 1) * 512]))
                p.s_ldx.wait(p.ve)
                p.s_mm.wait(p.ve)
                sc = 0.5 if half else 1.0
                p.s_vec.inc(p.ve.scalar_tensor_tensor(out=ob[:], in0=ps_o[:],
                                                      scalar=sc, in1=xb[:],
                                                      op0=ALU.mult, op1=ALU.add))
                p.s_vec.wait(p.ve)
                if y_dram is not None:
                    p.s_vec.inc(p.ve.scalar_tensor_tensor(
                        out=ob[:], in0=yb[:], scalar=0.5, in1=ob[:],
                        op0=ALU.mult, op1=ALU.add))
                    p.s_vec.wait(p.ve)
                p.s_vec.wait(p.sy)
                p.s_sto.dma(p.sy.dma_start(
                    out=x_out[co, :, tt * 512:(tt + 1) * 512], in_=ob[:]))
                p.s_sto.wait(p.ve)
                p.s_sto.wait(p.te)
            p.bar()
    p.pop()


def emit_att_softmax_heads(p, c, qkv_cache, y, z_dram):
    p.push()
    wbuf = p.sb([P, 3 * NC * HD], BF16)
    ps = [p.psum([P, 512]) for _ in range(2)]
    qb = p.sb([P, S], BF16)
    kb = p.sb([P, S], BF16)
    vb = p.sb([P, S], BF16)
    vT = p.sb([P, NC * HD], BF16)
    ps_sc = [p.psum([P, 512]) for _ in range(2)]
    ps_z = p.psum([P, 512])
    ps_r = p.psum([P, 512])
    ps_n = p.psum([1, 2 * 512])
    ps_t = ps_r[:].bitcast(BF16)[:, 0:P]
    probsT = p.sb([P, NC * 512], BF16)
    sq = p.sb([P, 512], F32)
    rows = p.sb([1, 4 * S], F32)
    negB = p.sb([1, S], BF16)
    kap = p.sb([1, 1], F32)
    zsb = p.sb([P, 512], BF16)

    for h in range(H):
        def evac(mi, tt, ps_):
            dst = (qb, kb, vb)[mi]
            p.s_mm.wait(p.ve)
            p.s_vec.inc(p.ve.tensor_copy(out=dst[:, tt * 512:(tt + 1) * 512],
                                         in_=ps_[:]))

        emit_qkv_head(p, qkv_cache, h, y, wbuf, ps, evac)
        p.bar()
        for sc_ in range(NC):
            p.s_mm.inc(p.te.transpose(out=ps_t, in_=vb[:, sc_ * P:(sc_ + 1) * P],
                                      identity=c["ident_b"][:]))
            p.s_mm.wait(p.se)
            p.s_act.inc(p.se.activation(out=vT[:, sc_ * P:(sc_ + 1) * P],
                                        in_=ps_t, func=AF.Copy))
            p.s_act.wait(p.te)
        p.bar()
        for which, buf in ((0, qb), (1, kb)):
            for tt in range(NT):
                p.s_act.inc(p.se.activation(out=sq[:],
                                            in_=buf[:, tt * 512:(tt + 1) * 512],
                                            func=AF.Square))
                p.s_act.wait(p.te)
                p.s_mm.inc(p.te.matmul(out=ps_n[:, which * 512:(which + 1) * 512],
                                       lhsT=c["ones_col_f"][:], rhs=sq[:],
                                       start=True, stop=True))
                p.s_mm.wait(p.ve)
                p.s_vec.inc(p.ve.tensor_copy(
                    out=rows[:, which * S + tt * 512: which * S + tt * 512 + 512],
                    in_=ps_n[:, which * 512:(which + 1) * 512]))
                p.s_vec.wait(p.se)
                p.s_vec.wait(p.te)
        p.bar()
        p.s_vec.inc(p.ve.reduce_max(out=kap[:], in_=rows[:, S:2 * S], axis=AX.X))
        p.s_vec.wait(p.ve)
        p.s_vec.inc(p.ve.tensor_scalar(out=kap[:], in0=kap[:], scalar1=1.0,
                                       scalar2=None, op0=ALU.mult))
        p.s_vec.wait(p.se)
        p.s_act.inc(p.se.activation(out=kap[:], in_=kap[:], func=AF.Sqrt))
        p.s_act.inc(p.se.activation(out=rows[:, 0:S], in_=rows[:, 0:S],
                                    func=AF.Sqrt))
        p.s_act.wait(p.ve)
        p.s_vec.inc(p.ve.tensor_scalar(out=negB[:], in0=rows[:, 0:S],
                                       scalar1=kap[:, 0:1], scalar2=-1.0,
                                       op0=ALU.mult, op1=ALU.mult))
        p.s_vec.wait(p.te)
        p.bar()
        for cq in range(NT):
            nsk = 4 * (cq + 1)
            for m in range(nsk):
                pb = m % 2
                p.s_act.wait(p.te)
                p.s_vec.wait(p.te)
                p.te.matmul(out=ps_sc[pb][:], lhsT=kb[:, m * P:(m + 1) * P],
                            rhs=qb[:, cq * 512:(cq + 1) * 512],
                            start=True, stop=False)
                p.s_mm.inc(p.te.matmul(out=ps_sc[pb][:], lhsT=c["ones_row_b"][:],
                                       rhs=negB[:, cq * 512:(cq + 1) * 512],
                                       start=False, stop=True))
                p.s_mm.wait(p.se)
                p.s_act.inc(p.se.activation(out=probsT[:, m * 512:(m + 1) * 512],
                                            in_=ps_sc[pb][:], func=AF.Exp,
                                            scale=SCALE_ATT))
                if m >= nsk - 4:
                    mi = m - (nsk - 4)
                    p.s_act.wait(p.ve)
                    p.s_vec.inc(p.ve.tensor_tensor(
                        out=probsT[:, m * 512:(m + 1) * 512],
                        in0=probsT[:, m * 512:(m + 1) * 512],
                        in1=c["masks_b"][:, mi * 512:(mi + 1) * 512], op=ALU.mult))
            p.bar()
            for m in range(nsk):
                p.te.matmul(out=ps_n[:, 0:512], lhsT=c["ones_col_b"][:],
                            rhs=probsT[:, m * 512:(m + 1) * 512],
                            start=(m == 0), stop=(m == nsk - 1))
            for m in range(nsk):
                ins = p.te.matmul(out=ps_z[:], lhsT=vT[:, m * HD:(m + 1) * HD],
                                  rhs=probsT[:, m * 512:(m + 1) * 512],
                                  start=(m == 0), stop=(m == nsk - 1))
            p.s_mm.inc(ins)
            p.s_mm.wait(p.ve)
            p.s_vec.inc(p.ve.tensor_scalar(out=rows[:, 3 * S:3 * S + 512],
                                           in0=ps_n[:, 0:512],
                                           scalar1=1e-30, scalar2=None, op0=ALU.add))
            p.s_vec.wait(p.ve)
            p.s_vec.inc(p.ve.reciprocal(out=rows[:, 3 * S:3 * S + 512],
                                        in_=rows[:, 3 * S:3 * S + 512]))
            p.s_vec.wait(p.te)
            p.s_mm.inc(p.te.matmul(out=ps_r[:], lhsT=c["ones_row_f"][:],
                                   rhs=rows[:, 3 * S:3 * S + 512],
                                   start=True, stop=True))
            p.s_mm.wait(p.se)
            p.s_act.inc(p.se.activation(out=zsb[:], in_=ps_z[:], func=AF.Copy))
            p.s_act.wait(p.ve)
            p.s_vec.inc(p.ve.tensor_tensor(out=zsb[:], in0=zsb[:], in1=ps_r[:],
                                           op=ALU.mult))
            p.s_vec.wait(p.sy)
            p.s_sto.dma(p.sy.dma_start(out=z_dram[h, :, cq * 512:(cq + 1) * 512],
                                       in_=zsb[:]))
            p.s_sto.wait(p.se)
            p.s_sto.wait(p.ve)
            p.s_sto.wait(p.te)
            p.bar()
    p.pop()


def emit_qkv_head_f32(p, qkv_ap, h, y32_dram, wbuf32, yrow, ps, evac):
    p.s_ldx.dma(p.sy.dma_start(out=wbuf32[:], in_=qkv_ap[h]))
    p.s_ldx.wait(p.te)
    mm_after_tt = []
    vec_after = []
    for tt in range(NT):
        b = tt % 2
        if tt >= 2:
            p.se.wait_ge(p.s_mm.h, mm_after_tt[tt - 2])
        for ch in range(NC):
            p.s_lds.dma(p.se.dma_start(
                out=yrow[b][:, ch * 512:(ch + 1) * 512],
                in_=y32_dram[ch, :, tt * 512:(tt + 1) * 512]))
        p.s_lds.wait(p.te)
        for mi in range(3):
            k = tt * 3 + mi
            if k >= 2:
                p.s_vec.wait(p.te, vec_after[k - 2])
            for ch in range(NC):
                ins = p.te.matmul(
                    out=ps[k % 2][:],
                    lhsT=wbuf32[:, (mi * NC + ch) * HD:(mi * NC + ch + 1) * HD],
                    rhs=yrow[b][:, ch * 512:(ch + 1) * 512],
                    start=(ch == 0), stop=(ch == NC - 1))
            p.s_mm.inc(ins)
            evac(mi, tt, ps[k % 2])
            vec_after.append(p.s_vec.n)
        mm_after_tt.append(p.s_mm.n)



def emit_att0_heads(p, c, g, y32_dram, att_dram):
    p.push()
    wbuf32 = p.sb([P, 3 * NC * HD], F32)
    yrow = [p.sb([P, NC * 512], F32) for _ in range(2)]
    ps = [p.psum([P, 512]) for _ in range(2)]
    qb = p.sb([P, S], BF16)
    kb = p.sb([P, S], BF16)
    vb = p.sb([P, S], BF16)
    kT = p.sb([P, NC * HD], BF16)
    vT = p.sb([P, NC * HD], BF16)
    ps_t = p.psum([P, P], BF16)
    ps_st = p.psum([P, 64])
    ps_kv = p.psum([P, P])
    ps_o = p.psum([P, 64])
    st_bf = p.sb([P, 64], BF16)
    Pst = p.sb([P, P], F32)
    Pbf = p.sb([P, P], BF16)
    ob = p.sb([P, 64], BF16)
    tmp = p.sb([P, 512], F32)

    for h in range(H):
        def evac(mi, tt, ps_):
            dst = (qb, kb, vb)[mi]
            sl_ = dst[:, tt * 512:(tt + 1) * 512]
            p.s_mm.wait(p.ve)
            if mi == 0:
                p.s_vec.inc(p.ve.tensor_scalar(out=sl_, in0=ps_[:],
                                               scalar1=SCALE_ATT, scalar2=None,
                                               op0=ALU.mult))
            else:
                p.s_vec.inc(p.ve.tensor_scalar(out=tmp[:], in0=ps_[:], scalar1=0.0,
                                               scalar2=2.0, op0=ALU.is_ge,
                                               op1=ALU.mult))
                p.s_vec.wait(p.ve)
                p.s_vec.inc(p.ve.tensor_scalar(out=sl_, in0=tmp[:], scalar1=-1.0,
                                               scalar2=None, op0=ALU.add))

        emit_qkv_head_f32(p, g["a0_qkv"], h, y32_dram, wbuf32, yrow, ps, evac)
        p.bar()
        for sc_ in range(NC):
            for src, dst in ((kb, kT), (vb, vT)):
                p.s_mm.inc(p.te.transpose(out=ps_t[:],
                                          in_=src[:, sc_ * P:(sc_ + 1) * P],
                                          identity=c["ident_b"][:]))
                p.s_mm.wait(p.se)
                p.s_act.inc(p.se.activation(out=dst[:, sc_ * P:(sc_ + 1) * P],
                                            in_=ps_t[:], func=AF.Copy))
                p.s_act.wait(p.te)
        p.s_act.wait(p.ve)
        p.s_vec.inc(p.ve.memset(Pst[:], 0.0))
        p.s_vec.inc(p.ve.memset(Pbf[:], 0.0))
        p.s_vec.wait(p.te)
        p.bar()
        for n in range(32):
            sc2 = n // 2
            base = (n % 2) * 64
            s0 = n * 64
            p.s_mm.inc(p.te.matmul(out=ps_st[base:base + 64, :],
                                   lhsT=kb[:, s0:s0 + 64],
                                   rhs=qb[:, s0:s0 + 64], start=True, stop=True))
            p.s_mm.wait(p.ve)
            p.s_vec.inc(p.ve.tensor_tensor(out=st_bf[base:base + 64, :],
                                           in0=ps_st[base:base + 64, :],
                                           in1=c["triu64_b"][base:base + 64, :],
                                           op=ALU.mult))
            p.s_vec.wait(p.te)
            p.s_mm.inc(p.te.matmul(
                out=ps_kv[:], lhsT=kT[base:base + 64, sc2 * HD:(sc2 + 1) * HD],
                rhs=vT[base:base + 64, sc2 * HD:(sc2 + 1) * HD],
                start=True, stop=True))
            p.te.matmul(out=ps_o[:],
                        lhsT=vT[base:base + 64, sc2 * HD:(sc2 + 1) * HD],
                        rhs=st_bf[base:base + 64, :], start=True, stop=False)
            p.s_mm.inc(p.te.matmul(out=ps_o[:], lhsT=Pbf[:],
                                   rhs=qb[:, s0:s0 + 64], start=False, stop=True))
            p.s_mm.wait(p.ve)
            p.s_mm.wait(p.se)
            p.s_vec.inc(p.ve.tensor_tensor(out=Pst[:], in0=Pst[:], in1=ps_kv[:],
                                           op=ALU.add))
            p.s_vec.wait(p.ve)
            p.s_vec.inc(p.ve.tensor_copy(out=Pbf[:], in_=Pst[:]))
            p.s_vec.wait(p.te)
            p.s_act.inc(p.se.activation(out=ob[:], in_=ps_o[:], func=AF.Copy))
            p.s_act.wait(p.sy)
            p.s_sto.dma(p.sy.dma_start(out=att_dram[h, :, s0:s0 + 64], in_=ob[:]))
            p.s_sto.wait(p.se)
            p.s_sto.wait(p.te)
        p.bar()
    p.pop()



def emit_ffn1(p, c, g, scr, wc, x_in, x_out):
    root = p.push()
    mhA = p.sb([R, S], BF16)
    b_sb = p.sb([R, F1], BF16)

    p.push()
    y = p.sb([P, NC * S], BF16)
    rstd_r = p.sb([1, S], F32)
    emit_ln(p, c, x_in, y, 0, S, rstd_r=rstd_r)
    p.s_sto.dma(p.sy.dma_start(out=scr["scr_r"], in_=rstd_r[:]))
    p.push()
    ps_rb = p.psum([P, 512])
    w1c = p.sb([P, NC], F32)
    p.s_ldx.dma(p.sy.dma_start(out=w1c[:], in_=g["f1_ln1"]))
    p.bar()
    p.s_ldx.wait(p.ve)
    for tt in range(NT):
        p.s_vec.wait(p.te)
        p.s_mm.inc(p.te.matmul(out=ps_rb[:], lhsT=c["ones_row_f"][:],
                               rhs=rstd_r[:, tt * 512:(tt + 1) * 512],
                               start=True, stop=True))
        p.s_mm.wait(p.ve)
        p.s_mm.wait(p.sy)
        for ch in range(NC):
            ysl = y[:, ch * S + tt * 512: ch * S + tt * 512 + 512]
            p.s_sto.dma(p.sy.dma_start(
                out=scr["scr_y"][ch, :, tt * 512:(tt + 1) * 512], in_=ysl))
            p.s_sto.wait(p.ve)
            p.s_vec.inc(p.ve.scalar_tensor_tensor(
                out=ysl, in0=ysl, scalar=w1c[:, ch:ch + 1], in1=ps_rb[:],
                op0=ALU.mult, op1=ALU.mult))
        p.s_vec.wait(p.sy)
        p.bar()
    p.pop()
    emit_att_softmax_heads(p, c, wc["qkv_f1"], y, scr["scr_z"])
    p.pop()
    p.bar()
    emit_oproj_residual(p, c, wc["wo_f1"], scr["scr_z"], None, None,
                        out_att=scr["scr_att"])
    p.bar()

    p.push()
    a_sb = p.sb([P, NC * R], BF16)
    ps_r16 = p.psum([R, 2 * 512])
    ps_rb = p.psum([P, 512])
    rowf = p.sb([R, 2 * 512], F32)
    rstd_r2 = p.sb([1, S], F32)
    w2c = p.sb([P, NC], F32)
    ytile = p.sb([P, 512], BF16)
    y2t = p.sb([P, 512], BF16)
    atile = p.sb([P, 512], BF16)
    p.s_ldx.dma(p.sy.dma_start(out=rstd_r2[:], in_=scr["scr_r"]))
    p.s_ldx.dma(p.sy.dma_start(out=w2c[:], in_=g["f1_ln2"]))
    p.s_ldx.dma(p.sy.dma_start(out=a_sb[:], in_=wc["a_b"]))
    p.s_ldx.dma(p.sy.dma_start(out=b_sb[:], in_=wc["b_b"]))
    p.bar()
    p.s_ldx.wait(p.ve)
    p.s_ldx.wait(p.te)
    for tt in range(NT):
        p.s_mm.inc(p.te.matmul(out=ps_rb[:], lhsT=c["ones_row_f"][:],
                               rhs=rstd_r2[:, tt * 512:(tt + 1) * 512],
                               start=True, stop=True))
        p.s_mm.wait(p.ve)
        for ch in range(NC):
            p.s_ldx.dma(p.sy.dma_start(
                out=ytile[:], in_=scr["scr_y"][ch, :, tt * 512:(tt + 1) * 512]))
            p.s_ldx.dma(p.sy.dma_start(
                out=atile[:], in_=scr["scr_att"][ch, :, tt * 512:(tt + 1) * 512]))
            p.s_ldx.wait(p.ve)
            p.s_vec.inc(p.ve.scalar_tensor_tensor(
                out=y2t[:], in0=ytile[:], scalar=w2c[:, ch:ch + 1], in1=ps_rb[:],
                op0=ALU.mult, op1=ALU.mult))
            p.s_vec.wait(p.te)
            p.te.matmul(out=ps_r16[:, 0:512], lhsT=a_sb[:, ch * R:(ch + 1) * R],
                        rhs=y2t[:], start=(ch == 0), stop=(ch == NC - 1))
            ins = p.te.matmul(out=ps_r16[:, 512:1024],
                              lhsT=a_sb[:, ch * R:(ch + 1) * R],
                              rhs=atile[:], start=(ch == 0), stop=(ch == NC - 1))
            p.s_mm.inc(ins)
            p.s_mm.wait(p.sy)
        p.s_mm.wait(p.ve)
        p.s_mm.wait(p.se)
        p.s_act.inc(p.se.activation(out=rowf[:, 512:1024], in_=ps_r16[:, 512:1024],
                                    func=AF.Tanh))
        p.s_act.wait(p.ve)
        p.s_vec.inc(p.ve.tensor_copy(out=rowf[:, 0:512], in_=ps_r16[:, 0:512]))
        p.s_vec.wait(p.ve)
        p.s_vec.inc(p.ve.scalar_tensor_tensor(
            out=mhA[:, tt * 512:(tt + 1) * 512], in0=rowf[:, 512:1024], scalar=0.1,
            in1=rowf[:, 0:512], op0=ALU.mult, op1=ALU.mult))
        p.s_vec.wait(p.te)
        p.bar()
    p.pop()
    p.bar()

    TB = 512
    MSLAB = 16
    NSLAB = F1 // (MSLAB * P)
    for t0 in range(0, S, TB):
        p.push()
        tt = t0 // TB
        y2 = p.sb([P, NC * TB], BF16)
        acc = p.sb([P, NC * TB], F32)
        inter = p.sb([P, MSLAB * TB], BF16)
        wgu_b = [p.sb([P, 2 * NC * P], BF16) for _ in range(3)]
        wd_slab = p.sb([P, MSLAB * D], BF16)
        ps_g = [p.psum([P, TB]) for _ in range(2)]
        ps_u = [p.psum([P, TB]) for _ in range(2)]
        ps_d = [p.psum([P, TB]) for _ in range(2)]
        ps_rb = p.psum([P, 512])
        gg = p.sb([P, 2 * TB], BF16)
        rstd_r2 = p.sb([1, S], F32)
        w2c = p.sb([P, NC], F32)
        p.s_ldx.dma(p.sy.dma_start(out=rstd_r2[:], in_=scr["scr_r"]))
        p.s_ldx.dma(p.sy.dma_start(out=w2c[:], in_=g["f1_ln2"]))
        p.bar()
        p.s_ldx.wait(p.te)
        p.s_ldx.wait(p.ve)
        p.s_mm.inc(p.te.matmul(out=ps_rb[:], lhsT=c["ones_row_f"][:],
                               rhs=rstd_r2[:, t0:t0 + TB], start=True, stop=True))
        p.s_mm.wait(p.ve)
        ytile = p.sb([P, 512], BF16)
        for ch in range(NC):
            p.s_ldx.dma(p.sy.dma_start(
                out=ytile[:], in_=scr["scr_y"][ch, :, t0:t0 + TB]))
            p.s_ldx.wait(p.ve)
            p.s_vec.inc(p.ve.scalar_tensor_tensor(
                out=y2[:, ch * TB:(ch + 1) * TB], in0=ytile[:],
                scalar=w2c[:, ch:ch + 1], in1=ps_rb[:], op0=ALU.mult, op1=ALU.mult))
            p.s_vec.wait(p.sy)
        p.bar()
        for sl_i in range(NSLAB):
            mm_after = []
            vec_after = []
            for jj in range(MSLAB):
                j = sl_i * MSLAB + jj
                b = jj % 3
                pb = jj % 2
                if jj >= 3:
                    p.sy.wait_ge(p.s_mm.h, mm_after[jj - 3])
                    p.po.wait_ge(p.s_mm.h, mm_after[jj - 3])
                p.s_ldx.dma(p.sy.dma_start(out=wgu_b[b][:, :NC * P],
                                           in_=wc["win"][j]))
                p.s_lds.dma(p.sy.dma_start(out=wd_slab[:, jj * D:(jj + 1) * D],
                                           in_=wc["wdn"][j]))
                p.s_ldg.dma(p.po.dma_start(out=wgu_b[b][:, NC * P:],
                                           in_=wc["wup"][j]))
                p.s_ldx.wait(p.te)
                p.s_ldg.wait(p.te)
                if jj >= 2:
                    p.s_vec.wait(p.te, vec_after[jj - 2])
                for ch in range(NC):
                    p.te.matmul(out=ps_g[pb][:],
                                lhsT=wgu_b[b][:, ch * P:(ch + 1) * P],
                                rhs=y2[:, ch * TB:(ch + 1) * TB],
                                start=(ch == 0), stop=False)
                p.te.matmul(out=ps_g[pb][:], lhsT=b_sb[:, j * P:(j + 1) * P],
                            rhs=mhA[:, t0:t0 + TB], start=False, stop=True)
                for ch in range(NC):
                    ins = p.te.matmul(
                        out=ps_u[pb][:],
                        lhsT=wgu_b[b][:, NC * P + ch * P: NC * P + (ch + 1) * P],
                        rhs=y2[:, ch * TB:(ch + 1) * TB],
                        start=(ch == 0), stop=(ch == NC - 1))
                p.s_mm.inc(ins)
                mm_after.append(p.s_mm.n)
                p.s_mm.wait(p.se)
                if jj >= 2:
                    p.s_vec.wait(p.se, vec_after[jj - 2])
                p.s_act.inc(p.se.activation(out=gg[:, pb * TB:(pb + 1) * TB],
                                            in_=ps_g[pb][:], func=AF.Silu))
                p.s_act.wait(p.ve)
                p.s_vec.inc(p.ve.tensor_tensor(out=inter[:, jj * TB:(jj + 1) * TB],
                                               in0=gg[:, pb * TB:(pb + 1) * TB],
                                               in1=ps_u[pb][:], op=ALU.mult))
                vec_after.append(p.s_vec.n)
            p.bar()
            p.s_lds.wait(p.te)
            dn_after = []
            for hc in range(NC):
                pb = hc % 2
                if hc >= 2:
                    p.te.wait_ge((p.s_act if sl_i == 0 else p.s_vec).h,
                                 dn_after[hc - 2])
                for jj in range(MSLAB):
                    ins = p.te.matmul(
                        out=ps_d[pb][:],
                        lhsT=wd_slab[:, jj * D + hc * P: jj * D + (hc + 1) * P],
                        rhs=inter[:, jj * TB:(jj + 1) * TB],
                        start=(jj == 0), stop=(jj == MSLAB - 1))
                p.s_mm.inc(ins)
                if sl_i == 0:
                    p.s_mm.wait(p.se)
                    p.s_act.inc(p.se.activation(out=acc[:, hc * TB:(hc + 1) * TB],
                                                in_=ps_d[pb][:], func=AF.Copy))
                    dn_after.append(p.s_act.n)
                else:
                    p.s_mm.wait(p.ve)
                    p.s_vec.inc(p.ve.tensor_tensor(out=acc[:, hc * TB:(hc + 1) * TB],
                                                   in0=acc[:, hc * TB:(hc + 1) * TB],
                                                   in1=ps_d[pb][:], op=ALU.add))
                    dn_after.append(p.s_vec.n)
            p.bar()

        xb = p.sb([P, 512], F32)
        yb2 = p.sb([P, 512], BF16)
        xo = p.sb([P, 512], F32)
        for ch in range(NC):
            p.s_ldx.dma(p.sy.dma_start(out=xb[:], in_=x_in[ch, :, t0:t0 + TB]))
            p.s_ldx.dma(p.sy.dma_start(out=yb2[:],
                                       in_=scr["scr_y"][ch, :, t0:t0 + TB]))
            p.s_ldx.wait(p.ve)
            p.s_vec.inc(p.ve.scalar_tensor_tensor(
                out=xo[:], in0=acc[:, ch * TB:(ch + 1) * TB], scalar=0.5,
                in1=xb[:], op0=ALU.mult, op1=ALU.add))
            p.s_vec.wait(p.ve)
            p.s_vec.inc(p.ve.scalar_tensor_tensor(out=xo[:], in0=yb2[:], scalar=0.5,
                                                  in1=xo[:], op0=ALU.mult,
                                                  op1=ALU.add))
            p.s_vec.wait(p.sy)
            p.s_sto.dma(p.sy.dma_start(out=x_out[ch, :, t0:t0 + TB], in_=xo[:]))
            p.s_sto.wait(p.ve)
            p.bar()
        p.pop()
        p.bar()
    p.pop()



W_DTYPES = {"gup_packedT": I32, "down_packedT": I32}


def declare_dram(nc, w, names):
    g = {}
    for n in names:
        arr = w[n]
        dt = W_DTYPES.get(n, F32)
        g[n] = nc.dram_tensor(n, list(arr.shape), dt, kind="ExternalInput").ap()
    return g


CONST_NAMES = ["ones_col", "ones_row", "ident", "diag_masks", "triu64"]
FFN0_NAMES = ["gup_packedT", "gup_scalesT", "down_packedT", "down_scalesT"]
ATT0_NAMES = ["a0_qkv", "a0_wo"]
ATT1_NAMES = ["a1_qkv", "a1_wo"]
FFN1_NAMES = ["f1_qkv", "f1_wo", "f1_a", "f1_b", "f1_win",
              "f1_wup", "f1_wdn", "f1_ln1", "f1_ln2"]
ALL_NAMES = CONST_NAMES + FFN0_NAMES + ATT0_NAMES + ATT1_NAMES + FFN1_NAMES


def make_ffn0_cache(nc):
    return {
        "wg": nc.dram_tensor("cache_wg", [M0 // P, P, NC * P], BF16).ap(),
        "wu": nc.dram_tensor("cache_wu", [M0 // P, P, NC * P], BF16).ap(),
        "wd": nc.dram_tensor("cache_wd", [128, P, D], BF16).ap(),
    }


def make_scr(nc):
    return {
        "scr_y": nc.dram_tensor("scr_y", [NC, P, S], BF16).ap(),
        "scr_z": nc.dram_tensor("scr_z", [NC, P, S], BF16).ap(),
        "scr_att": nc.dram_tensor("scr_att", [NC, P, S], BF16).ap(),
        "scr_r": nc.dram_tensor("scr_r", [1, S], F32).ap(),
        "scr_y32": nc.dram_tensor("scr_y32", [NC, P, S], F32).ap(),
    }


def make_wc(nc):
    return {
        "qkv_a1": nc.dram_tensor("wc_qkv_a1", [H, P, 3 * NC * HD], BF16).ap(),
        "qkv_f1": nc.dram_tensor("wc_qkv_f1", [H, P, 3 * NC * HD], BF16).ap(),
        "wo_a0": nc.dram_tensor("wc_wo_a0", [NC, P, D], BF16).ap(),
        "wo_a1": nc.dram_tensor("wc_wo_a1", [NC, P, D], BF16).ap(),
        "wo_f1": nc.dram_tensor("wc_wo_f1", [NC, P, D], BF16).ap(),
        "win": nc.dram_tensor("wc_win", [F1 // P, P, NC * P], BF16).ap(),
        "wup": nc.dram_tensor("wc_wup", [F1 // P, P, NC * P], BF16).ap(),
        "wdn": nc.dram_tensor("wc_wdn", [64, P, D], BF16).ap(),
        "a_b": nc.dram_tensor("wc_a_b", [P, NC * R], BF16).ap(),
        "b_b": nc.dram_tensor("wc_b_b", [R, F1], BF16).ap(),
    }


def finish(p):
    p.bar()
    p.sy.wait_ge(p.s_sto.h, p.s_sto.n)
    p.se.wait_ge(p.s_sto2.h, p.s_sto2.n)
    p.bar()


def _new_nc():
    return bass.Bass("TRN2", target_bir_lowering=False)


def build_fused(w):
    nc = _new_nc()
    g = declare_dram(nc, w, ALL_NAMES)
    x_in = nc.dram_tensor("x_in", [NC, P, S], F32, kind="ExternalInput").ap()
    x_out = nc.dram_tensor("x_out", [NC, P, S], F32, kind="ExternalOutput").ap()
    xa = nc.dram_tensor("x_a", [NC, P, S], F32).ap()
    xb = nc.dram_tensor("x_b", [NC, P, S], F32).ap()
    cache = make_ffn0_cache(nc)
    scr = make_scr(nc)
    wc = make_wc(nc)
    with ExitStack() as ctx:
        p = Prog(nc, ctx)
        c = emit_const_load(p, g)
        emit_weight_prep(p, g, wc)
        emit_ffn0_dequant(p, g, cache)
        emit_ffn0_compute(p, c, g, cache, x_in, xa)
        p.push()
        emit_ln(p, c, xa, None, 0, S, y32_dram=scr["scr_y32"])
        p.bar()
        emit_att0_heads(p, c, g, scr["scr_y32"], scr["scr_z"])
        p.pop()
        p.bar()
        emit_oproj_residual(p, c, wc["wo_a0"], scr["scr_z"], xa, xb)
        p.bar()
        emit_ffn0_compute(p, c, g, cache, xb, xa)
        emit_ffn1(p, c, g, scr, wc, xa, xb)
        p.push()
        y = p.sb([P, NC * S], BF16)
        emit_ln(p, c, xb, y, 0, S)
        emit_att_softmax_heads(p, c, wc["qkv_a1"], y, scr["scr_z"])
        p.pop()
        p.bar()
        emit_oproj_residual(p, c, wc["wo_a1"], scr["scr_z"], xb, xa)
        p.bar()
        emit_ffn1(p, c, g, scr, wc, xa, x_out)
        finish(p)
    return nc


def kernel(**inputs):
    w = prep_weights(inputs)
    xs = [prep_x(inputs["L0_attn_x"], b) for b in range(8)]
    nc = build_fused(w)
    in_maps = []
    for b in range(8):
        m = {n: w[n] for n in ALL_NAMES}
        m["x_in"] = xs[b]
        in_maps.append(m)
    res = run_bass_kernel_spmd(nc, in_maps, core_ids=list(range(8)))
    return np.stack([post_out(np.asarray(res.results[b]["x_out"]))
                     for b in range(8)]).astype(np.float32)


# version 06 (final)
# speedup vs baseline: 1.3167x; 1.1844x over previous
import numpy as np
from contextlib import ExitStack

import concourse.bass as bass
import concourse.mybir as mybir
from concourse.bass_utils import run_bass_kernel_spmd

F32 = mybir.dt.float32
BF16 = mybir.dt.bfloat16
I32 = mybir.dt.int32
F16 = mybir.dt.float16
AF = mybir.ActivationFunctionType
ALU = mybir.AluOpType
AX = mybir.AxisListType

P = 128
D = 2048
S = 2048
NC = D // P
H = 16
HD = 128
M0 = 16384
F1 = 8192
R = 16
SCALE_ATT = float(1.0 / np.sqrt(128.0))
NT = S // 512

PI = np.concatenate([np.arange(0, D, 2), np.arange(1, D, 2)])
SIG = np.concatenate([np.arange(0, M0, 2), np.arange(1, M0, 2)])


def _ct(a):
    return np.ascontiguousarray(a)



def prep_weights(ins):
    w = {}
    pi = PI
    gp = np.asarray(ins["L0_ffn_gate_W_packed"])
    up = np.asarray(ins["L0_ffn_up_W_packed"])
    w["gup_packedT"] = _ct(np.stack([gp[SIG].T, up[SIG].T]).reshape(2, 8, P, M0))
    gs = np.asarray(ins["L0_ffn_gate_scales"])
    us = np.asarray(ins["L0_ffn_up_scales"])
    w["gup_scalesT"] = _ct(np.stack([gs[SIG].T, us[SIG].T]))
    dp = np.asarray(ins["L0_ffn_down_W_packed"])
    w["down_packedT"] = _ct(dp[pi].T.reshape(64, P, D))
    ds = np.asarray(ins["L0_ffn_down_scales"])
    w["down_scalesT"] = _ct(ds[pi].T)

    def prepT(a):
        return _ct(np.asarray(a).T[pi].reshape(NC, P, np.asarray(a).shape[0]))

    def dd_head_tile0(a):
        return (np.asarray(a).T[pi].reshape(NC, P, H, HD)
                .transpose(2, 1, 0, 3).reshape(H, P, NC * HD))

    w["a0_qkv"] = _ct(np.concatenate(
        [dd_head_tile0(ins[f"L0_attn_W{s}"]) for s in "qkv"], axis=2))
    w["a0_wo"] = _ct(np.asarray(ins["L0_attn_Wo"])[pi, :].T.reshape(NC, P, D))

    def head_tile(a):
        return a.reshape(H, NC, P, HD).transpose(0, 2, 1, 3).reshape(H, P, NC * HD)

    wq1 = np.asarray(ins["L1_attn_W_Q"]).transpose(0, 2, 1)[:, pi, :]
    wk1 = np.asarray(ins["L1_attn_W_K"]).transpose(0, 2, 1)[:, pi, :]
    wv1 = np.asarray(ins["L1_attn_W_V"]).transpose(0, 2, 1)[:, pi, :]
    wo1 = np.asarray(ins["L1_attn_W_O"])
    w["a1_qkv"] = _ct(np.concatenate([head_tile(wq1), head_tile(wk1),
                                      head_tile(wv1)], axis=2))
    w["a1_wo"] = _ct(wo1[:, pi, :].transpose(0, 2, 1))

    def dd_head_tile(a):
        return (np.asarray(a).T[pi].reshape(NC, P, H, HD)
                .transpose(2, 1, 0, 3).reshape(H, P, NC * HD))
    w["f1_qkv"] = _ct(np.concatenate(
        [dd_head_tile(ins[f"L1_ffn_W{s}"]) for s in "qkv"], axis=2))
    w["f1_wo"] = _ct(np.asarray(ins["L1_ffn_Wo"])[pi, :].T.reshape(NC, P, D))
    w["f1_a"] = _ct(np.asarray(ins["L1_ffn_A"])[pi].reshape(NC, P, R)
                    .transpose(1, 0, 2).reshape(P, NC * R))
    w["f1_b"] = _ct(np.asarray(ins["L1_ffn_B"]))
    def jtile(a):
        return _ct(np.asarray(a).T[pi].reshape(NC, P, F1 // P, P)
                   .transpose(2, 1, 0, 3).reshape(F1 // P, P, NC * P))
    w["f1_win"] = jtile(ins["L1_ffn_W_in_base"])
    w["f1_wup"] = jtile(ins["L1_ffn_W_up_base"])
    w["f1_wdn"] = _ct(np.asarray(ins["L1_ffn_W_down"])[pi, :].T.reshape(64, P, D))
    w["f1_ln1"] = _ct(np.asarray(ins["L1_ffn_ln1_w"])[pi].reshape(NC, P).T)
    w["f1_ln2"] = _ct(np.asarray(ins["L1_ffn_ln2_w"])[pi].reshape(NC, P).T)

    w["ones_col"] = np.ones((P, 1), np.float32)
    w["ones_row"] = np.ones((1, P), np.float32)
    w["ident"] = np.eye(P, dtype=np.float32)
    masks = np.zeros((4, P, 512), np.float32)
    for m in range(4):
        masks[m] = (m * 128 + np.arange(P)[:, None]) <= np.arange(512)[None, :]
    w["diag_masks"] = masks
    w["triu64"] = _ct(np.tile(np.triu(np.ones((64, 64), np.float32)), (2, 1)))
    return w


def prep_x(x_full, core):
    return _ct(np.asarray(x_full)[core].T[PI].reshape(NC, P, S))


def post_out(x_out):
    inv = np.argsort(PI)
    return _ct(np.asarray(x_out, np.float32).reshape(D, S)[inv].T)



class Sem:
    def __init__(self, ctx, nc, name):
        self.h = ctx.enter_context(nc.semaphore(name))
        self.n = 0

    def inc(self, instr, k=1):
        instr.then_inc(self.h, k)
        self.n += k

    def dma(self, instr):
        self.inc(instr, 16)

    def wait(self, eng, n=None):
        eng.wait_ge(self.h, self.n if n is None else n)


class Prog:
    def __init__(self, nc, ctx):
        self.nc = nc
        self.ctx = ctx
        self._stack = [ctx]
        self.te = nc.tensor
        self.ve = nc.vector
        self.se = nc.scalar
        self.po = nc.gpsimd
        self.sy = nc.sync
        for name in ["ldx", "lds", "ldg", "mm", "act", "vec", "pool", "sto", "sto2"]:
            setattr(self, "s_" + name, Sem(ctx, nc, name))
        self.uid = 0

    def push(self):
        st = ExitStack()
        self._stack.append(st)
        return st

    def pop(self):
        self._stack.pop().close()

    def _c(self):
        return self._stack[-1]

    def name(self, base):
        self.uid += 1
        return f"{base}{self.uid}"

    def sb(self, shape, dt):
        return self._c().enter_context(self.nc.sbuf_tensor(self.name("t"), shape, dt))

    def psum(self, shape, dt=F32):
        return self._c().enter_context(self.nc.psum_tensor(self.name("q"), shape, dt))

    def bar(self):
        self.nc.all_engine_barrier()


def emit_const_load(p, g):
    c = {}
    c["ones_col_f"] = p.sb([P, 1], F32)
    c["ones_row_f"] = p.sb([1, P], F32)
    c["ones_row_b"] = p.sb([1, P], BF16)
    c["ones_col_b"] = p.sb([P, 1], BF16)
    c["ident_b"] = p.sb([P, P], BF16)
    c["masks_b"] = p.sb([P, 4 * 512], BF16)
    c["triu64_b"] = p.sb([P, 64], BF16)
    p.s_ldx.dma(p.sy.dma_start(out=c["ones_col_f"][:], in_=g["ones_col"]))
    p.s_ldx.dma(p.sy.dma_start(out=c["ones_row_f"][:], in_=g["ones_row"]))
    p.s_ldg.dma(p.po.dma_start(out=c["ones_col_b"][:], in_=g["ones_col"]))
    p.s_ldg.dma(p.po.dma_start(out=c["ones_row_b"][:], in_=g["ones_row"]))
    p.s_ldg.dma(p.po.dma_start(out=c["ident_b"][:], in_=g["ident"]))
    p.s_ldg.dma(p.po.dma_start(
        out=c["masks_b"][:].rearrange("p (m f) -> p m f", m=4),
        in_=g["diag_masks"].transpose([1, 0, 2])))
    p.s_ldg.dma(p.po.dma_start(out=c["triu64_b"][:], in_=g["triu64"]))
    p.bar()
    p.s_ldx.wait(p.ve)
    p.s_ldg.wait(p.ve)
    p.bar()
    return c


def emit_ln(p, c, x_dram, y_sb, t0, tw, rstd_r=None, x_keep=None, y32_dram=None):
    p.push()
    xb = x_keep if x_keep is not None else p.sb([P, NC * 512], F32)
    sqb = p.sb([P, 2 * 512], F32)
    ps_stat = p.psum([1, 2 * 512])
    ps_bc = p.psum([P, 2 * 512])
    rowbuf = p.sb([1, 6 * 512], F32)
    tmp = p.sb([P, 512], F32)
    tmp32 = p.sb([P, 2 * 512], F32) if y32_dram is not None else None
    sto_after = []
    for st in range(tw // 512):
        tt = t0 + st * 512

        def xt(ch):
            if x_keep is not None:
                return xb[:, ch * tw + st * 512: ch * tw + st * 512 + 512]
            return xb[:, ch * 512:(ch + 1) * 512]

        for ch in range(NC):
            p.s_ldx.dma(p.sy.dma_start(out=xt(ch), in_=x_dram[ch, :, tt:tt + 512]))
        p.bar()
        p.s_ldx.wait(p.te)
        p.s_ldx.wait(p.se)
        mm_after = []
        for ch in range(NC):
            sq = sqb[:, (ch % 2) * 512:(ch % 2) * 512 + 512]
            p.te.matmul(out=ps_stat[:, 0:512], lhsT=c["ones_col_f"][:], rhs=xt(ch),
                        start=(ch == 0), stop=(ch == NC - 1))
            if ch >= 2:
                p.s_mm.wait(p.se, mm_after[ch - 2])
            p.s_act.inc(p.se.activation(out=sq, in_=xt(ch), func=AF.Square))
            p.s_act.wait(p.te)
            ins = p.te.matmul(out=ps_stat[:, 512:1024], lhsT=c["ones_col_f"][:],
                              rhs=sq, start=(ch == 0), stop=(ch == NC - 1))
            p.s_mm.inc(ins)
            mm_after.append(p.s_mm.n)
        p.s_mm.wait(p.ve)
        mean = rowbuf[:, 0:512]
        var = rowbuf[:, 512:1024]
        t2 = rowbuf[:, 1024:1536]
        rstd = rowbuf[:, 1536:2048]
        cc = rowbuf[:, 2048:2560]
        t5 = rowbuf[:, 2560:3072]
        p.s_vec.inc(p.ve.tensor_scalar(out=mean, in0=ps_stat[:, 0:512],
                                       scalar1=1.0 / D, scalar2=None, op0=ALU.mult))
        p.s_vec.inc(p.ve.tensor_scalar(out=var, in0=ps_stat[:, 512:1024],
                                       scalar1=1.0 / D, scalar2=None, op0=ALU.mult))
        p.s_vec.wait(p.ve)
        p.s_vec.inc(p.ve.tensor_tensor(out=t2, in0=mean, in1=mean, op=ALU.mult))
        p.s_vec.wait(p.ve)
        p.s_vec.inc(p.ve.tensor_tensor(out=var, in0=var, in1=t2, op=ALU.subtract))
        p.s_vec.wait(p.ve)
        p.s_vec.inc(p.ve.tensor_scalar(out=t2, in0=var, scalar1=1e-5,
                                       scalar2=None, op0=ALU.add))
        p.s_vec.wait(p.se)
        p.s_act.inc(p.se.activation(out=t2, in_=t2, func=AF.Sqrt))
        p.s_act.wait(p.ve)
        p.s_vec.inc(p.ve.reciprocal(out=rstd, in_=t2))
        p.s_vec.wait(p.ve)
        p.s_vec.inc(p.ve.tensor_tensor(out=t2, in0=mean, in1=rstd, op=ALU.mult))
        p.s_vec.wait(p.ve)
        p.s_vec.inc(p.ve.tensor_scalar(out=cc, in0=t2, scalar1=-1.0,
                                       scalar2=None, op0=ALU.mult))
        if rstd_r is not None:
            rr = rstd_r[:, st * 512:(st + 1) * 512]
            p.s_vec.inc(p.ve.tensor_scalar(out=t5, in0=var, scalar1=1e-5,
                                           scalar2=None, op0=ALU.add))
            p.s_vec.wait(p.ve)
            p.s_vec.inc(p.ve.reciprocal(out=t5, in_=t5))
            p.s_vec.wait(p.ve)
            p.s_vec.inc(p.ve.tensor_tensor(out=t5, in0=var, in1=t5, op=ALU.mult))
            p.s_vec.wait(p.ve)
            p.s_vec.inc(p.ve.tensor_scalar(out=t5, in0=t5, scalar1=1e-6,
                                           scalar2=None, op0=ALU.add))
            p.s_vec.wait(p.se)
            p.s_act.inc(p.se.activation(out=t5, in_=t5, func=AF.Sqrt))
            p.s_act.wait(p.ve)
            p.s_vec.inc(p.ve.reciprocal(out=rr, in_=t5))
        p.s_vec.wait(p.te)
        p.te.matmul(out=ps_bc[:, 0:512], lhsT=c["ones_row_f"][:], rhs=rstd,
                    start=True, stop=True)
        p.s_mm.inc(p.te.matmul(out=ps_bc[:, 512:1024], lhsT=c["ones_row_f"][:],
                               rhs=cc, start=True, stop=True))
        p.s_mm.wait(p.ve)
        for ch in range(NC):
            p.s_vec.inc(p.ve.tensor_tensor(out=tmp[:], in0=xt(ch),
                                           in1=ps_bc[:, 0:512], op=ALU.mult))
            p.s_vec.wait(p.ve)
            if y32_dram is not None:
                y32t = tmp32[:, (ch % 2) * 512:(ch % 2) * 512 + 512]
                if len(sto_after) >= 2:
                    p.s_sto.wait(p.ve, sto_after[-2])
                p.s_vec.inc(p.ve.tensor_tensor(out=y32t, in0=tmp[:],
                                               in1=ps_bc[:, 512:1024], op=ALU.add))
                p.s_vec.wait(p.sy)
                p.s_sto.dma(p.sy.dma_start(
                    out=y32_dram[ch, :, tt:tt + 512], in_=y32t))
                sto_after.append(p.s_sto.n)
                if y_sb is not None:
                    ins = p.ve.tensor_scalar(
                        out=y_sb[:, ch * tw + st * 512: ch * tw + st * 512 + 512],
                        in0=y32t, scalar1=1.0, scalar2=None, op0=ALU.mult)
                    p.s_vec.inc(ins)
            else:
                ins = p.ve.tensor_tensor(
                    out=y_sb[:, ch * tw + st * 512: ch * tw + st * 512 + 512],
                    in0=tmp[:], in1=ps_bc[:, 512:1024], op=ALU.add)
                if ch == NC - 1:
                    p.s_vec.inc(ins)
        p.bar()
    p.pop()



def _dequant_tile(p, eng, vi, s_eff, out_lo, out_hi, scr):
    F_ = vi.shape[1]
    vh, h32, mag, sg2 = (scr[k] for k in ["vh", "h32", "mag", "sg2"])
    eng.tensor_scalar(out=h32[:, :F_], in0=vi, scalar1=7, scalar2=25,
                      op0=ALU.bitwise_and, op1=ALU.logical_shift_left)
    eng.tensor_tensor(out=mag[:, :F_], in0=h32[:, :F_].bitcast(F16)[:, 1::2],
                      in1=s_eff, op=ALU.mult)
    eng.tensor_scalar(out=sg2[:, :F_], in0=vi, scalar1=2, scalar2=2,
                      op0=ALU.logical_shift_right, op1=ALU.bitwise_and)
    eng.scalar_tensor_tensor(out=out_lo, in0=sg2[:, :F_], scalar=1.0,
                             in1=mag[:, :F_], op0=ALU.subtract, op1=ALU.mult)
    eng.tensor_scalar(out=vh[:, :F_], in0=vi, scalar1=4, scalar2=None,
                      op0=ALU.logical_shift_right)
    eng.tensor_scalar(out=h32[:, :F_], in0=vh[:, :F_], scalar1=7, scalar2=25,
                      op0=ALU.bitwise_and, op1=ALU.logical_shift_left)
    eng.tensor_tensor(out=mag[:, :F_], in0=h32[:, :F_].bitcast(F16)[:, 1::2],
                      in1=s_eff, op=ALU.mult)
    eng.tensor_scalar(out=sg2[:, :F_], in0=vh[:, :F_], scalar1=2, scalar2=2,
                      op0=ALU.logical_shift_right, op1=ALU.bitwise_and)
    return eng.scalar_tensor_tensor(out=out_hi, in0=sg2[:, :F_], scalar=1.0,
                                    in1=mag[:, :F_], op0=ALU.subtract, op1=ALU.mult)


def emit_ffn0_dequant(p, g, cache):
    st_ = p.push()
    MS = 2048
    lanes = []
    for li, (eng, qsem, ldq, ldsem, stq, stsem) in enumerate([
            (p.ve, p.s_vec, p.sy, p.s_ldx, p.sy, p.s_sto),
            (p.ve, p.s_pool, p.se, p.s_lds, p.se, p.s_sto2)]):
        scr = {k: p.sb([P, MS], I32) for k in ["vh", "h32", "sg2"]}
        scr["mag"] = p.sb([P, MS], F32)
        lanes.append(dict(
            eng=eng, qsem=qsem, ldq=ldq, ldsem=ldsem, stq=stq, stsem=stsem,
            scr=scr, vi=[p.sb([P, MS], I32) for _ in range(2)],
            sl=[p.sb([P, MS], F32) for _ in range(2)],
            ol=[p.sb([P, MS], BF16) for _ in range(2)],
            oh=[p.sb([P, MS], BF16) for _ in range(2)],
            i=0))

    jobs = []
    nj = MS // P
    for mat, wname in ((0, "wg"), (1, "wu")):
        for kc in range(8):
            for ms in range(0, M0, MS):
                j0 = ms // P
                dlo = cache[wname][j0:j0 + nj, :, kc * P:(kc + 1) * P]\
                    .transpose([1, 0, 2])
                dhi = cache[wname][j0:j0 + nj, :, (kc + 8) * P:(kc + 9) * P]\
                    .transpose([1, 0, 2])
                jobs.append((g["gup_packedT"][mat, kc, :, ms:ms + MS],
                             g["gup_scalesT"][mat, 8 * kc:8 * kc + 8, ms:ms + MS],
                             dlo, dhi))
    for mc in range(64):
        jobs.append((g["down_packedT"][mc], g["down_scalesT"][8 * mc:8 * mc + 8, :],
                     cache["wd"][mc], cache["wd"][mc + 64]))

    for gi, (src, srows, dlo, dhi) in enumerate(jobs):
        L = lanes[gi % 2]
        i, b = L["i"], L["i"] % 2
        eng = L["eng"]
        Fw = src.shape[-1]
        if i >= 2:
            L["ldq"].wait_ge(L["qsem"].h, i - 1)
        L["ldsem"].dma(L["ldq"].dma_start(out=L["vi"][b][:, :Fw], in_=src))
        L["ldsem"].dma(L["ldq"].dma_start(
            out=L["sl"][b][:, :Fw],
            in_=srows.unsqueeze(1).broadcast_to([8, 16, Fw])))
        L["ldsem"].wait(eng)
        if i >= 2:
            L["stsem"].wait(eng, 32 * (i - 1))
        eng.tensor_scalar(out=L["sl"][b][:, :Fw], in0=L["sl"][b][:, :Fw],
                          scalar1=-16384.0, scalar2=None, op0=ALU.mult)
        ins = _dequant_tile(p, eng, L["vi"][b][:, :Fw], L["sl"][b][:, :Fw],
                            L["ol"][b][:, :Fw], L["oh"][b][:, :Fw], L["scr"])
        L["qsem"].inc(ins)
        L["qsem"].wait(L["stq"])
        L["stsem"].dma(L["stq"].dma_start(out=dlo, in_=L["ol"][b][:, :Fw]))
        L["stsem"].dma(L["stq"].dma_start(out=dhi, in_=L["oh"][b][:, :Fw]))
        L["i"] += 1
    p.bar()
    p.pop()


def emit_ffn0_compute(p, c, g, cache, x_in, x_out):
    TB = 512
    MSLAB = 16
    NSLAB = M0 // (MSLAB * P)
    for t0 in range(0, S, TB):
        st_ = p.push()
        y = p.sb([P, NC * TB], BF16)
        emit_ln(p, c, x_in, y, t0, TB)
        acc = p.sb([P, NC * TB], F32)
        inter = p.sb([P, MSLAB * TB], BF16)
        wgu_b = [p.sb([P, 2 * NC * P], BF16) for _ in range(3)]
        wd_slab = p.sb([P, MSLAB * D], BF16)
        ps_g = [p.psum([P, TB]) for _ in range(2)]
        ps_u = [p.psum([P, TB]) for _ in range(2)]
        ps_d = [p.psum([P, TB]) for _ in range(2)]
        gg = p.sb([P, 2 * TB], BF16)
        for sl_i in range(NSLAB):
            mm_after = []
            vec_after = []
            for jj in range(MSLAB):
                j = sl_i * MSLAB + jj
                b = jj % 3
                pb = jj % 2
                if jj >= 3:
                    p.sy.wait_ge(p.s_mm.h, mm_after[jj - 3])
                    p.po.wait_ge(p.s_mm.h, mm_after[jj - 3])
                p.s_ldx.dma(p.sy.dma_start(out=wgu_b[b][:, :NC * P],
                                           in_=cache["wg"][j]))
                p.s_lds.dma(p.sy.dma_start(out=wd_slab[:, jj * D:(jj + 1) * D],
                                           in_=cache["wd"][j]))
                p.s_ldg.dma(p.po.dma_start(out=wgu_b[b][:, NC * P:],
                                           in_=cache["wu"][j]))
                p.s_ldx.wait(p.te)
                p.s_ldg.wait(p.te)
                if jj >= 2:
                    p.s_vec.wait(p.te, vec_after[jj - 2])
                for ch in range(NC):
                    p.te.matmul(out=ps_g[pb][:],
                                lhsT=wgu_b[b][:, ch * P:(ch + 1) * P],
                                rhs=y[:, ch * TB:(ch + 1) * TB],
                                start=(ch == 0), stop=(ch == NC - 1))
                for ch in range(NC):
                    ins = p.te.matmul(out=ps_u[pb][:],
                                      lhsT=wgu_b[b][:, NC * P + ch * P: NC * P + (ch + 1) * P],
                                      rhs=y[:, ch * TB:(ch + 1) * TB],
                                      start=(ch == 0), stop=(ch == NC - 1))
                p.s_mm.inc(ins)
                mm_after.append(p.s_mm.n)
                p.s_mm.wait(p.se)
                if jj >= 2:
                    p.s_vec.wait(p.se, vec_after[jj - 2])
                p.s_act.inc(p.se.activation(out=gg[:, pb * TB:(pb + 1) * TB],
                                            in_=ps_g[pb][:],
                                            func=AF.Gelu_apprx_tanh))
                p.s_act.wait(p.ve)
                p.s_vec.inc(p.ve.tensor_tensor(out=inter[:, jj * TB:(jj + 1) * TB],
                                               in0=gg[:, pb * TB:(pb + 1) * TB],
                                               in1=ps_u[pb][:], op=ALU.mult))
                vec_after.append(p.s_vec.n)
            p.bar()
            p.s_lds.wait(p.te)
            dn_after = []
            for hc in range(NC):
                pb = hc % 2
                if hc >= 2:
                    p.te.wait_ge((p.s_act if sl_i == 0 else p.s_vec).h,
                                 dn_after[hc - 2])
                for jj in range(MSLAB):
                    ins = p.te.matmul(
                        out=ps_d[pb][:],
                        lhsT=wd_slab[:, jj * D + hc * P: jj * D + (hc + 1) * P],
                        rhs=inter[:, jj * TB:(jj + 1) * TB],
                        start=(jj == 0), stop=(jj == MSLAB - 1))
                p.s_mm.inc(ins)
                if sl_i == 0:
                    p.s_mm.wait(p.se)
                    p.s_act.inc(p.se.activation(out=acc[:, hc * TB:(hc + 1) * TB],
                                                in_=ps_d[pb][:], func=AF.Copy))
                    dn_after.append(p.s_act.n)
                else:
                    p.s_mm.wait(p.ve)
                    p.s_vec.inc(p.ve.tensor_tensor(out=acc[:, hc * TB:(hc + 1) * TB],
                                                   in0=acc[:, hc * TB:(hc + 1) * TB],
                                                   in1=ps_d[pb][:], op=ALU.add))
                    dn_after.append(p.s_vec.n)
            p.bar()

        xb = p.sb([P, 512], F32)
        xo = p.sb([P, 512], F32)
        for ch in range(NC):
            p.s_ldx.dma(p.sy.dma_start(out=xb[:], in_=x_in[ch, :, t0:t0 + TB]))
            p.s_ldx.wait(p.ve)
            p.s_vec.inc(p.ve.scalar_tensor_tensor(out=xo[:], in0=acc[:, ch * TB:(ch + 1) * TB],
                                                  scalar=0.5, in1=xb[:],
                                                  op0=ALU.mult, op1=ALU.add))
            p.s_vec.wait(p.sy)
            p.s_sto.dma(p.sy.dma_start(out=x_out[ch, :, t0:t0 + TB], in_=xo[:]))
            p.s_sto.wait(p.ve)
            p.bar()
        p.pop()
        p.bar()



def emit_weight_prep(p, g, wc):
    p.push()
    stg = [p.sb([P, F1], F32) for _ in range(2)]
    ob = [p.sb([P, F1], BF16) for _ in range(2)]
    jobs = []
    for h in range(H):
        jobs.append((g["a1_qkv"][h], wc["qkv_a1"][h], P, 3 * NC * HD))
        jobs.append((g["f1_qkv"][h], wc["qkv_f1"][h], P, 3 * NC * HD))
    for tag, sn in (("wo_a0", "a0_wo"), ("wo_a1", "a1_wo"), ("wo_f1", "f1_wo")):
        for ci in range(NC):
            jobs.append((g[sn][ci], wc[tag][ci], P, D))
    for tag, sn in (("win", "f1_win"), ("wup", "f1_wup")):
        for j in range(F1 // P):
            jobs.append((g[sn][j], wc[tag][j], P, NC * P))
    for j in range(64):
        jobs.append((g["f1_wdn"][j], wc["wdn"][j], P, D))
    jobs.append((g["f1_a"], wc["a_b"][:], P, NC * R))
    jobs.append((g["f1_b"], wc["b_b"][:], R, F1))

    acts = []
    for i, (src, dst, np_, width) in enumerate(jobs):
        b = i % 2
        q = p.sy if b == 0 else p.se
        lsem = p.s_ldx if b == 0 else p.s_lds
        ssem = p.s_sto if b == 0 else p.s_sto2
        if i >= 2:
            q.wait_ge(p.s_act.h, acts[i - 2])
        lsem.dma(q.dma_start(out=stg[b][:np_, :width], in_=src))
        lsem.wait(p.se)
        p.s_act.inc(p.se.activation(out=ob[b][:np_, :width],
                                    in_=stg[b][:np_, :width], func=AF.Copy))
        acts.append(p.s_act.n)
        p.s_act.wait(q)
        ssem.dma(q.dma_start(out=dst, in_=ob[b][:np_, :width]))
    p.bar()
    p.pop()


def emit_qkv_head(p, qkv_cache, h, y, wbuf, ps, evac):
    p.s_ldx.dma(p.sy.dma_start(out=wbuf[:], in_=qkv_cache[h]))
    p.s_ldx.wait(p.te)
    vec_after = []
    for mi in range(3):
        for tt in range(NT):
            k = mi * NT + tt
            if k >= 2:
                p.s_vec.wait(p.te, vec_after[k - 2])
            for ch in range(NC):
                ins = p.te.matmul(
                    out=ps[k % 2][:],
                    lhsT=wbuf[:, (mi * NC + ch) * HD:(mi * NC + ch + 1) * HD],
                    rhs=y[:, ch * S + tt * 512: ch * S + tt * 512 + 512],
                    start=(ch == 0), stop=(ch == NC - 1))
            p.s_mm.inc(ins)
            evac(mi, tt, ps[k % 2])
            vec_after.append(p.s_vec.n)


def emit_oproj_residual(p, c, wo_cache, z_dram, x_in, x_out,
                        half=False, y_dram=None, out_att=None):
    p.push()
    zrow = p.sb([P, NC * 512], BF16)
    wo_all = p.sb([P, NC * D], BF16)
    ps_o = p.psum([P, 512])
    xb = p.sb([P, 512], F32)
    yb = p.sb([P, 512], BF16)
    ob = p.sb([P, 512], F32)
    obb = p.sb([P, 512], BF16)
    for ci in range(NC):
        p.s_ldx.dma(p.sy.dma_start(out=wo_all[:, ci * D:(ci + 1) * D],
                                   in_=wo_cache[ci]))
    p.bar()
    p.s_ldx.wait(p.te)
    for tt in range(NT):
        for ch in range(NC):
            p.s_ldx.dma(p.sy.dma_start(out=zrow[:, ch * 512:(ch + 1) * 512],
                                       in_=z_dram[ch, :, tt * 512:(tt + 1) * 512]))
        p.bar()
        p.s_ldx.wait(p.te)
        for co in range(NC):
            for ci in range(NC):
                ins = p.te.matmul(out=ps_o[:],
                                  lhsT=wo_all[:, ci * D + co * P: ci * D + (co + 1) * P],
                                  rhs=zrow[:, ci * 512:(ci + 1) * 512],
                                  start=(ci == 0), stop=(ci == NC - 1))
            p.s_mm.inc(ins)
            if out_att is not None:
                p.s_mm.wait(p.se)
                p.s_act.inc(p.se.activation(out=obb[:], in_=ps_o[:], func=AF.Copy))
                p.s_act.wait(p.sy)
                p.s_sto.dma(p.sy.dma_start(
                    out=out_att[co, :, tt * 512:(tt + 1) * 512], in_=obb[:]))
                p.s_sto.wait(p.se)
                p.s_sto.wait(p.te)
            else:
                p.s_ldx.dma(p.sy.dma_start(out=xb[:],
                                           in_=x_in[co, :, tt * 512:(tt + 1) * 512]))
                if y_dram is not None:
                    p.s_ldx.dma(p.sy.dma_start(
                        out=yb[:], in_=y_dram[co, :, tt * 512:(tt + 1) * 512]))
                p.s_ldx.wait(p.ve)
                p.s_mm.wait(p.ve)
                sc = 0.5 if half else 1.0
                p.s_vec.inc(p.ve.scalar_tensor_tensor(out=ob[:], in0=ps_o[:],
                                                      scalar=sc, in1=xb[:],
                                                      op0=ALU.mult, op1=ALU.add))
                p.s_vec.wait(p.ve)
                if y_dram is not None:
                    p.s_vec.inc(p.ve.scalar_tensor_tensor(
                        out=ob[:], in0=yb[:], scalar=0.5, in1=ob[:],
                        op0=ALU.mult, op1=ALU.add))
                    p.s_vec.wait(p.ve)
                p.s_vec.wait(p.sy)
                p.s_sto.dma(p.sy.dma_start(
                    out=x_out[co, :, tt * 512:(tt + 1) * 512], in_=ob[:]))
                p.s_sto.wait(p.ve)
                p.s_sto.wait(p.te)
            p.bar()
    p.pop()


def emit_att_softmax_heads(p, c, qkv_cache, y, z_dram):
    p.push()
    wbuf = p.sb([P, 3 * NC * HD], BF16)
    ps = [p.psum([P, 512]) for _ in range(2)]
    qb = p.sb([P, S], BF16)
    kb = p.sb([P, S], BF16)
    vb = p.sb([P, S], BF16)
    vT = p.sb([P, NC * HD], BF16)
    ps_sc = [p.psum([P, 512]) for _ in range(2)]
    ps_z = p.psum([P, 512])
    ps_r = p.psum([P, 512])
    ps_n = p.psum([1, 2 * 512])
    ps_t = ps_r[:].bitcast(BF16)[:, 0:P]
    probsT = p.sb([P, NC * 512], BF16)
    sq = p.sb([P, 512], F32)
    rows = p.sb([1, 4 * S], F32)
    negB = p.sb([1, S], BF16)
    kap = p.sb([1, 1], F32)
    zsb = p.sb([P, 512], BF16)

    for h in range(H):
        def evac(mi, tt, ps_):
            dst = (qb, kb, vb)[mi]
            p.s_mm.wait(p.ve)
            p.s_vec.inc(p.ve.tensor_copy(out=dst[:, tt * 512:(tt + 1) * 512],
                                         in_=ps_[:]))

        emit_qkv_head(p, qkv_cache, h, y, wbuf, ps, evac)
        p.bar()
        for sc_ in range(NC):
            p.s_mm.inc(p.te.transpose(out=ps_t, in_=vb[:, sc_ * P:(sc_ + 1) * P],
                                      identity=c["ident_b"][:]))
            p.s_mm.wait(p.se)
            p.s_act.inc(p.se.activation(out=vT[:, sc_ * P:(sc_ + 1) * P],
                                        in_=ps_t, func=AF.Copy))
            p.s_act.wait(p.te)
        p.bar()
        for which, buf in ((0, qb), (1, kb)):
            for tt in range(NT):
                p.s_act.inc(p.se.activation(out=sq[:],
                                            in_=buf[:, tt * 512:(tt + 1) * 512],
                                            func=AF.Square))
                p.s_act.wait(p.te)
                p.s_mm.inc(p.te.matmul(out=ps_n[:, which * 512:(which + 1) * 512],
                                       lhsT=c["ones_col_f"][:], rhs=sq[:],
                                       start=True, stop=True))
                p.s_mm.wait(p.ve)
                p.s_vec.inc(p.ve.tensor_copy(
                    out=rows[:, which * S + tt * 512: which * S + tt * 512 + 512],
                    in_=ps_n[:, which * 512:(which + 1) * 512]))
                p.s_vec.wait(p.se)
                p.s_vec.wait(p.te)
        p.bar()
        p.s_vec.inc(p.ve.reduce_max(out=kap[:], in_=rows[:, S:2 * S], axis=AX.X))
        p.s_vec.wait(p.ve)
        p.s_vec.inc(p.ve.tensor_scalar(out=kap[:], in0=kap[:], scalar1=1.0,
                                       scalar2=None, op0=ALU.mult))
        p.s_vec.wait(p.se)
        p.s_act.inc(p.se.activation(out=kap[:], in_=kap[:], func=AF.Sqrt))
        p.s_act.inc(p.se.activation(out=rows[:, 0:S], in_=rows[:, 0:S],
                                    func=AF.Sqrt))
        p.s_act.wait(p.ve)
        p.s_vec.inc(p.ve.tensor_scalar(out=negB[:], in0=rows[:, 0:S],
                                       scalar1=kap[:, 0:1], scalar2=-1.0,
                                       op0=ALU.mult, op1=ALU.mult))
        p.s_vec.wait(p.te)
        p.bar()
        for cq in range(NT):
            nsk = 4 * (cq + 1)
            act_after = []
            for m in range(nsk):
                pb = m % 2
                if m >= 2:
                    p.s_act.wait(p.te, act_after[m - 2])
                p.te.matmul(out=ps_sc[pb][:], lhsT=kb[:, m * P:(m + 1) * P],
                            rhs=qb[:, cq * 512:(cq + 1) * 512],
                            start=True, stop=False)
                p.s_mm.inc(p.te.matmul(out=ps_sc[pb][:], lhsT=c["ones_row_b"][:],
                                       rhs=negB[:, cq * 512:(cq + 1) * 512],
                                       start=False, stop=True))
                p.s_mm.wait(p.se)
                p.s_act.inc(p.se.activation(out=probsT[:, m * 512:(m + 1) * 512],
                                            in_=ps_sc[pb][:], func=AF.Exp,
                                            scale=SCALE_ATT))
                act_after.append(p.s_act.n)
                if m >= nsk - 4:
                    mi = m - (nsk - 4)
                    p.s_act.wait(p.ve)
                    p.s_vec.inc(p.ve.tensor_tensor(
                        out=probsT[:, m * 512:(m + 1) * 512],
                        in0=probsT[:, m * 512:(m + 1) * 512],
                        in1=c["masks_b"][:, mi * 512:(mi + 1) * 512], op=ALU.mult))
            p.bar()
            for m in range(nsk):
                p.te.matmul(out=ps_n[:, 0:512], lhsT=c["ones_col_b"][:],
                            rhs=probsT[:, m * 512:(m + 1) * 512],
                            start=(m == 0), stop=(m == nsk - 1))
            for m in range(nsk):
                ins = p.te.matmul(out=ps_z[:], lhsT=vT[:, m * HD:(m + 1) * HD],
                                  rhs=probsT[:, m * 512:(m + 1) * 512],
                                  start=(m == 0), stop=(m == nsk - 1))
            p.s_mm.inc(ins)
            p.s_mm.wait(p.ve)
            p.s_vec.inc(p.ve.tensor_scalar(out=rows[:, 3 * S:3 * S + 512],
                                           in0=ps_n[:, 0:512],
                                           scalar1=1e-30, scalar2=None, op0=ALU.add))
            p.s_vec.wait(p.ve)
            p.s_vec.inc(p.ve.reciprocal(out=rows[:, 3 * S:3 * S + 512],
                                        in_=rows[:, 3 * S:3 * S + 512]))
            p.s_vec.wait(p.te)
            p.s_mm.inc(p.te.matmul(out=ps_r[:], lhsT=c["ones_row_f"][:],
                                   rhs=rows[:, 3 * S:3 * S + 512],
                                   start=True, stop=True))
            p.s_mm.wait(p.se)
            p.s_act.inc(p.se.activation(out=zsb[:], in_=ps_z[:], func=AF.Copy))
            p.s_act.wait(p.ve)
            p.s_vec.inc(p.ve.tensor_tensor(out=zsb[:], in0=zsb[:], in1=ps_r[:],
                                           op=ALU.mult))
            p.s_vec.wait(p.sy)
            p.s_sto.dma(p.sy.dma_start(out=z_dram[h, :, cq * 512:(cq + 1) * 512],
                                       in_=zsb[:]))
            p.s_sto.wait(p.se)
            p.s_sto.wait(p.ve)
            p.s_sto.wait(p.te)
            p.bar()
    p.pop()


def emit_qkv_head_f32(p, qkv_ap, h, y32_dram, wbuf32, yrow, ps, evac):
    p.s_ldx.dma(p.sy.dma_start(out=wbuf32[:], in_=qkv_ap[h]))
    p.s_ldx.wait(p.te)
    mm_after_tt = []
    vec_after = []
    for tt in range(NT):
        b = tt % 2
        if tt >= 2:
            p.se.wait_ge(p.s_mm.h, mm_after_tt[tt - 2])
        for ch in range(NC):
            p.s_lds.dma(p.se.dma_start(
                out=yrow[b][:, ch * 512:(ch + 1) * 512],
                in_=y32_dram[ch, :, tt * 512:(tt + 1) * 512]))
        p.s_lds.wait(p.te)
        for mi in range(3):
            k = tt * 3 + mi
            if k >= 2:
                p.s_vec.wait(p.te, vec_after[k - 2])
            for ch in range(NC):
                ins = p.te.matmul(
                    out=ps[k % 2][:],
                    lhsT=wbuf32[:, (mi * NC + ch) * HD:(mi * NC + ch + 1) * HD],
                    rhs=yrow[b][:, ch * 512:(ch + 1) * 512],
                    start=(ch == 0), stop=(ch == NC - 1))
            p.s_mm.inc(ins)
            evac(mi, tt, ps[k % 2])
            vec_after.append(p.s_vec.n)
        mm_after_tt.append(p.s_mm.n)



def emit_att0_heads(p, c, g, y32_dram, att_dram):
    p.push()
    wbuf32 = p.sb([P, 3 * NC * HD], F32)
    yrow = [p.sb([P, NC * 512], F32) for _ in range(2)]
    ps = [p.psum([P, 512]) for _ in range(2)]
    qb = p.sb([P, S], BF16)
    kb = p.sb([P, S], BF16)
    vb = p.sb([P, S], BF16)
    kT = p.sb([P, NC * HD], BF16)
    vT = p.sb([P, NC * HD], BF16)
    ps_t = p.psum([P, P], BF16)
    ps_st = p.psum([P, 64])
    ps_kv = p.psum([P, P])
    ps_o = p.psum([P, 64])
    st_bf = p.sb([P, 64], BF16)
    Pst = p.sb([P, P], F32)
    Pbf = p.sb([P, P], BF16)
    ob = p.sb([P, 64], BF16)
    tmp = p.sb([P, 512], F32)

    for h in range(H):
        def evac(mi, tt, ps_):
            dst = (qb, kb, vb)[mi]
            sl_ = dst[:, tt * 512:(tt + 1) * 512]
            p.s_mm.wait(p.ve)
            if mi == 0:
                p.s_vec.inc(p.ve.tensor_scalar(out=sl_, in0=ps_[:],
                                               scalar1=SCALE_ATT, scalar2=None,
                                               op0=ALU.mult))
            else:
                p.s_vec.inc(p.ve.tensor_scalar(out=tmp[:], in0=ps_[:], scalar1=0.0,
                                               scalar2=2.0, op0=ALU.is_ge,
                                               op1=ALU.mult))
                p.s_vec.wait(p.ve)
                p.s_vec.inc(p.ve.tensor_scalar(out=sl_, in0=tmp[:], scalar1=-1.0,
                                               scalar2=None, op0=ALU.add))

        emit_qkv_head_f32(p, g["a0_qkv"], h, y32_dram, wbuf32, yrow, ps, evac)
        p.bar()
        for sc_ in range(NC):
            for src, dst in ((kb, kT), (vb, vT)):
                p.s_mm.inc(p.te.transpose(out=ps_t[:],
                                          in_=src[:, sc_ * P:(sc_ + 1) * P],
                                          identity=c["ident_b"][:]))
                p.s_mm.wait(p.se)
                p.s_act.inc(p.se.activation(out=dst[:, sc_ * P:(sc_ + 1) * P],
                                            in_=ps_t[:], func=AF.Copy))
                p.s_act.wait(p.te)
        p.s_act.wait(p.ve)
        p.s_vec.inc(p.ve.memset(Pst[:], 0.0))
        p.s_vec.inc(p.ve.memset(Pbf[:], 0.0))
        p.s_vec.wait(p.te)
        p.bar()
        for n in range(32):
            sc2 = n // 2
            base = (n % 2) * 64
            s0 = n * 64
            p.s_mm.inc(p.te.matmul(out=ps_st[base:base + 64, :],
                                   lhsT=kb[:, s0:s0 + 64],
                                   rhs=qb[:, s0:s0 + 64], start=True, stop=True))
            p.s_mm.wait(p.ve)
            p.s_vec.inc(p.ve.tensor_tensor(out=st_bf[base:base + 64, :],
                                           in0=ps_st[base:base + 64, :],
                                           in1=c["triu64_b"][base:base + 64, :],
                                           op=ALU.mult))
            p.s_vec.wait(p.te)
            p.s_mm.inc(p.te.matmul(
                out=ps_kv[:], lhsT=kT[base:base + 64, sc2 * HD:(sc2 + 1) * HD],
                rhs=vT[base:base + 64, sc2 * HD:(sc2 + 1) * HD],
                start=True, stop=True))
            p.te.matmul(out=ps_o[:],
                        lhsT=vT[base:base + 64, sc2 * HD:(sc2 + 1) * HD],
                        rhs=st_bf[base:base + 64, :], start=True, stop=False)
            p.s_mm.inc(p.te.matmul(out=ps_o[:], lhsT=Pbf[:],
                                   rhs=qb[:, s0:s0 + 64], start=False, stop=True))
            p.s_mm.wait(p.ve)
            p.s_mm.wait(p.se)
            p.s_vec.inc(p.ve.tensor_tensor(out=Pst[:], in0=Pst[:], in1=ps_kv[:],
                                           op=ALU.add))
            p.s_vec.wait(p.ve)
            p.s_vec.inc(p.ve.tensor_copy(out=Pbf[:], in_=Pst[:]))
            p.s_vec.wait(p.te)
            p.s_act.inc(p.se.activation(out=ob[:], in_=ps_o[:], func=AF.Copy))
            p.s_act.wait(p.sy)
            p.s_sto.dma(p.sy.dma_start(out=att_dram[h, :, s0:s0 + 64], in_=ob[:]))
            p.s_sto.wait(p.se)
            p.s_sto.wait(p.te)
        p.bar()
    p.pop()



def emit_ffn1(p, c, g, scr, wc, x_in, x_out):
    root = p.push()
    mhA = p.sb([R, S], BF16)
    b_sb = p.sb([R, F1], BF16)

    p.push()
    y = p.sb([P, NC * S], BF16)
    rstd_r = p.sb([1, S], F32)
    emit_ln(p, c, x_in, y, 0, S, rstd_r=rstd_r)
    p.s_sto.dma(p.sy.dma_start(out=scr["scr_r"], in_=rstd_r[:]))
    p.push()
    ps_rb = p.psum([P, 512])
    w1c = p.sb([P, NC], F32)
    p.s_ldx.dma(p.sy.dma_start(out=w1c[:], in_=g["f1_ln1"]))
    p.bar()
    p.s_ldx.wait(p.ve)
    for tt in range(NT):
        p.s_vec.wait(p.te)
        p.s_mm.inc(p.te.matmul(out=ps_rb[:], lhsT=c["ones_row_f"][:],
                               rhs=rstd_r[:, tt * 512:(tt + 1) * 512],
                               start=True, stop=True))
        p.s_mm.wait(p.ve)
        p.s_mm.wait(p.sy)
        for ch in range(NC):
            ysl = y[:, ch * S + tt * 512: ch * S + tt * 512 + 512]
            p.s_sto.dma(p.sy.dma_start(
                out=scr["scr_y"][ch, :, tt * 512:(tt + 1) * 512], in_=ysl))
            p.s_sto.wait(p.ve)
            p.s_vec.inc(p.ve.scalar_tensor_tensor(
                out=ysl, in0=ysl, scalar=w1c[:, ch:ch + 1], in1=ps_rb[:],
                op0=ALU.mult, op1=ALU.mult))
        p.s_vec.wait(p.sy)
        p.bar()
    p.pop()
    emit_att_softmax_heads(p, c, wc["qkv_f1"], y, scr["scr_z"])
    p.pop()
    p.bar()
    emit_oproj_residual(p, c, wc["wo_f1"], scr["scr_z"], None, None,
                        out_att=scr["scr_att"])
    p.bar()

    p.push()
    a_sb = p.sb([P, NC * R], BF16)
    ps_r16 = p.psum([R, 2 * 512])
    ps_rb = p.psum([P, 512])
    rowf = p.sb([R, 2 * 512], F32)
    rstd_r2 = p.sb([1, S], F32)
    w2c = p.sb([P, NC], F32)
    ytile = p.sb([P, 512], BF16)
    y2t = p.sb([P, 512], BF16)
    atile = p.sb([P, 512], BF16)
    p.s_ldx.dma(p.sy.dma_start(out=rstd_r2[:], in_=scr["scr_r"]))
    p.s_ldx.dma(p.sy.dma_start(out=w2c[:], in_=g["f1_ln2"]))
    p.s_ldx.dma(p.sy.dma_start(out=a_sb[:], in_=wc["a_b"]))
    p.s_ldx.dma(p.sy.dma_start(out=b_sb[:], in_=wc["b_b"]))
    p.bar()
    p.s_ldx.wait(p.ve)
    p.s_ldx.wait(p.te)
    for tt in range(NT):
        p.s_mm.inc(p.te.matmul(out=ps_rb[:], lhsT=c["ones_row_f"][:],
                               rhs=rstd_r2[:, tt * 512:(tt + 1) * 512],
                               start=True, stop=True))
        p.s_mm.wait(p.ve)
        for ch in range(NC):
            p.s_ldx.dma(p.sy.dma_start(
                out=ytile[:], in_=scr["scr_y"][ch, :, tt * 512:(tt + 1) * 512]))
            p.s_ldx.dma(p.sy.dma_start(
                out=atile[:], in_=scr["scr_att"][ch, :, tt * 512:(tt + 1) * 512]))
            p.s_ldx.wait(p.ve)
            p.s_vec.inc(p.ve.scalar_tensor_tensor(
                out=y2t[:], in0=ytile[:], scalar=w2c[:, ch:ch + 1], in1=ps_rb[:],
                op0=ALU.mult, op1=ALU.mult))
            p.s_vec.wait(p.te)
            p.te.matmul(out=ps_r16[:, 0:512], lhsT=a_sb[:, ch * R:(ch + 1) * R],
                        rhs=y2t[:], start=(ch == 0), stop=(ch == NC - 1))
            ins = p.te.matmul(out=ps_r16[:, 512:1024],
                              lhsT=a_sb[:, ch * R:(ch + 1) * R],
                              rhs=atile[:], start=(ch == 0), stop=(ch == NC - 1))
            p.s_mm.inc(ins)
            p.s_mm.wait(p.sy)
        p.s_mm.wait(p.ve)
        p.s_mm.wait(p.se)
        p.s_act.inc(p.se.activation(out=rowf[:, 512:1024], in_=ps_r16[:, 512:1024],
                                    func=AF.Tanh))
        p.s_act.wait(p.ve)
        p.s_vec.inc(p.ve.tensor_copy(out=rowf[:, 0:512], in_=ps_r16[:, 0:512]))
        p.s_vec.wait(p.ve)
        p.s_vec.inc(p.ve.scalar_tensor_tensor(
            out=mhA[:, tt * 512:(tt + 1) * 512], in0=rowf[:, 512:1024], scalar=0.1,
            in1=rowf[:, 0:512], op0=ALU.mult, op1=ALU.mult))
        p.s_vec.wait(p.te)
        p.bar()
    p.pop()
    p.bar()

    TB = 512
    MSLAB = 16
    NSLAB = F1 // (MSLAB * P)
    for t0 in range(0, S, TB):
        p.push()
        tt = t0 // TB
        y2 = p.sb([P, NC * TB], BF16)
        acc = p.sb([P, NC * TB], F32)
        inter = p.sb([P, MSLAB * TB], BF16)
        wgu_b = [p.sb([P, 2 * NC * P], BF16) for _ in range(3)]
        wd_slab = p.sb([P, MSLAB * D], BF16)
        ps_g = [p.psum([P, TB]) for _ in range(2)]
        ps_u = [p.psum([P, TB]) for _ in range(2)]
        ps_d = [p.psum([P, TB]) for _ in range(2)]
        ps_rb = p.psum([P, 512])
        gg = p.sb([P, 2 * TB], BF16)
        rstd_r2 = p.sb([1, S], F32)
        w2c = p.sb([P, NC], F32)
        p.s_ldx.dma(p.sy.dma_start(out=rstd_r2[:], in_=scr["scr_r"]))
        p.s_ldx.dma(p.sy.dma_start(out=w2c[:], in_=g["f1_ln2"]))
        p.bar()
        p.s_ldx.wait(p.te)
        p.s_ldx.wait(p.ve)
        p.s_mm.inc(p.te.matmul(out=ps_rb[:], lhsT=c["ones_row_f"][:],
                               rhs=rstd_r2[:, t0:t0 + TB], start=True, stop=True))
        p.s_mm.wait(p.ve)
        ytile = p.sb([P, 512], BF16)
        for ch in range(NC):
            p.s_ldx.dma(p.sy.dma_start(
                out=ytile[:], in_=scr["scr_y"][ch, :, t0:t0 + TB]))
            p.s_ldx.wait(p.ve)
            p.s_vec.inc(p.ve.scalar_tensor_tensor(
                out=y2[:, ch * TB:(ch + 1) * TB], in0=ytile[:],
                scalar=w2c[:, ch:ch + 1], in1=ps_rb[:], op0=ALU.mult, op1=ALU.mult))
            p.s_vec.wait(p.sy)
        p.bar()
        for sl_i in range(NSLAB):
            mm_after = []
            vec_after = []
            for jj in range(MSLAB):
                j = sl_i * MSLAB + jj
                b = jj % 3
                pb = jj % 2
                if jj >= 3:
                    p.sy.wait_ge(p.s_mm.h, mm_after[jj - 3])
                    p.po.wait_ge(p.s_mm.h, mm_after[jj - 3])
                p.s_ldx.dma(p.sy.dma_start(out=wgu_b[b][:, :NC * P],
                                           in_=wc["win"][j]))
                p.s_lds.dma(p.sy.dma_start(out=wd_slab[:, jj * D:(jj + 1) * D],
                                           in_=wc["wdn"][j]))
                p.s_ldg.dma(p.po.dma_start(out=wgu_b[b][:, NC * P:],
                                           in_=wc["wup"][j]))
                p.s_ldx.wait(p.te)
                p.s_ldg.wait(p.te)
                if jj >= 2:
                    p.s_vec.wait(p.te, vec_after[jj - 2])
                for ch in range(NC):
                    p.te.matmul(out=ps_g[pb][:],
                                lhsT=wgu_b[b][:, ch * P:(ch + 1) * P],
                                rhs=y2[:, ch * TB:(ch + 1) * TB],
                                start=(ch == 0), stop=False)
                p.te.matmul(out=ps_g[pb][:], lhsT=b_sb[:, j * P:(j + 1) * P],
                            rhs=mhA[:, t0:t0 + TB], start=False, stop=True)
                for ch in range(NC):
                    ins = p.te.matmul(
                        out=ps_u[pb][:],
                        lhsT=wgu_b[b][:, NC * P + ch * P: NC * P + (ch + 1) * P],
                        rhs=y2[:, ch * TB:(ch + 1) * TB],
                        start=(ch == 0), stop=(ch == NC - 1))
                p.s_mm.inc(ins)
                mm_after.append(p.s_mm.n)
                p.s_mm.wait(p.se)
                if jj >= 2:
                    p.s_vec.wait(p.se, vec_after[jj - 2])
                p.s_act.inc(p.se.activation(out=gg[:, pb * TB:(pb + 1) * TB],
                                            in_=ps_g[pb][:], func=AF.Silu))
                p.s_act.wait(p.ve)
                p.s_vec.inc(p.ve.tensor_tensor(out=inter[:, jj * TB:(jj + 1) * TB],
                                               in0=gg[:, pb * TB:(pb + 1) * TB],
                                               in1=ps_u[pb][:], op=ALU.mult))
                vec_after.append(p.s_vec.n)
            p.bar()
            p.s_lds.wait(p.te)
            dn_after = []
            for hc in range(NC):
                pb = hc % 2
                if hc >= 2:
                    p.te.wait_ge((p.s_act if sl_i == 0 else p.s_vec).h,
                                 dn_after[hc - 2])
                for jj in range(MSLAB):
                    ins = p.te.matmul(
                        out=ps_d[pb][:],
                        lhsT=wd_slab[:, jj * D + hc * P: jj * D + (hc + 1) * P],
                        rhs=inter[:, jj * TB:(jj + 1) * TB],
                        start=(jj == 0), stop=(jj == MSLAB - 1))
                p.s_mm.inc(ins)
                if sl_i == 0:
                    p.s_mm.wait(p.se)
                    p.s_act.inc(p.se.activation(out=acc[:, hc * TB:(hc + 1) * TB],
                                                in_=ps_d[pb][:], func=AF.Copy))
                    dn_after.append(p.s_act.n)
                else:
                    p.s_mm.wait(p.ve)
                    p.s_vec.inc(p.ve.tensor_tensor(out=acc[:, hc * TB:(hc + 1) * TB],
                                                   in0=acc[:, hc * TB:(hc + 1) * TB],
                                                   in1=ps_d[pb][:], op=ALU.add))
                    dn_after.append(p.s_vec.n)
            p.bar()

        xb = p.sb([P, 512], F32)
        yb2 = p.sb([P, 512], BF16)
        xo = p.sb([P, 512], F32)
        for ch in range(NC):
            p.s_ldx.dma(p.sy.dma_start(out=xb[:], in_=x_in[ch, :, t0:t0 + TB]))
            p.s_ldx.dma(p.sy.dma_start(out=yb2[:],
                                       in_=scr["scr_y"][ch, :, t0:t0 + TB]))
            p.s_ldx.wait(p.ve)
            p.s_vec.inc(p.ve.scalar_tensor_tensor(
                out=xo[:], in0=acc[:, ch * TB:(ch + 1) * TB], scalar=0.5,
                in1=xb[:], op0=ALU.mult, op1=ALU.add))
            p.s_vec.wait(p.ve)
            p.s_vec.inc(p.ve.scalar_tensor_tensor(out=xo[:], in0=yb2[:], scalar=0.5,
                                                  in1=xo[:], op0=ALU.mult,
                                                  op1=ALU.add))
            p.s_vec.wait(p.sy)
            p.s_sto.dma(p.sy.dma_start(out=x_out[ch, :, t0:t0 + TB], in_=xo[:]))
            p.s_sto.wait(p.ve)
            p.bar()
        p.pop()
        p.bar()
    p.pop()



W_DTYPES = {"gup_packedT": I32, "down_packedT": I32}


def declare_dram(nc, w, names):
    g = {}
    for n in names:
        arr = w[n]
        dt = W_DTYPES.get(n, F32)
        g[n] = nc.dram_tensor(n, list(arr.shape), dt, kind="ExternalInput").ap()
    return g


CONST_NAMES = ["ones_col", "ones_row", "ident", "diag_masks", "triu64"]
FFN0_NAMES = ["gup_packedT", "gup_scalesT", "down_packedT", "down_scalesT"]
ATT0_NAMES = ["a0_qkv", "a0_wo"]
ATT1_NAMES = ["a1_qkv", "a1_wo"]
FFN1_NAMES = ["f1_qkv", "f1_wo", "f1_a", "f1_b", "f1_win",
              "f1_wup", "f1_wdn", "f1_ln1", "f1_ln2"]
ALL_NAMES = CONST_NAMES + FFN0_NAMES + ATT0_NAMES + ATT1_NAMES + FFN1_NAMES


def make_ffn0_cache(nc):
    return {
        "wg": nc.dram_tensor("cache_wg", [M0 // P, P, NC * P], BF16).ap(),
        "wu": nc.dram_tensor("cache_wu", [M0 // P, P, NC * P], BF16).ap(),
        "wd": nc.dram_tensor("cache_wd", [128, P, D], BF16).ap(),
    }


def make_scr(nc):
    return {
        "scr_y": nc.dram_tensor("scr_y", [NC, P, S], BF16).ap(),
        "scr_z": nc.dram_tensor("scr_z", [NC, P, S], BF16).ap(),
        "scr_att": nc.dram_tensor("scr_att", [NC, P, S], BF16).ap(),
        "scr_r": nc.dram_tensor("scr_r", [1, S], F32).ap(),
        "scr_y32": nc.dram_tensor("scr_y32", [NC, P, S], F32).ap(),
    }


def make_wc(nc):
    return {
        "qkv_a1": nc.dram_tensor("wc_qkv_a1", [H, P, 3 * NC * HD], BF16).ap(),
        "qkv_f1": nc.dram_tensor("wc_qkv_f1", [H, P, 3 * NC * HD], BF16).ap(),
        "wo_a0": nc.dram_tensor("wc_wo_a0", [NC, P, D], BF16).ap(),
        "wo_a1": nc.dram_tensor("wc_wo_a1", [NC, P, D], BF16).ap(),
        "wo_f1": nc.dram_tensor("wc_wo_f1", [NC, P, D], BF16).ap(),
        "win": nc.dram_tensor("wc_win", [F1 // P, P, NC * P], BF16).ap(),
        "wup": nc.dram_tensor("wc_wup", [F1 // P, P, NC * P], BF16).ap(),
        "wdn": nc.dram_tensor("wc_wdn", [64, P, D], BF16).ap(),
        "a_b": nc.dram_tensor("wc_a_b", [P, NC * R], BF16).ap(),
        "b_b": nc.dram_tensor("wc_b_b", [R, F1], BF16).ap(),
    }


def finish(p):
    p.bar()
    p.sy.wait_ge(p.s_sto.h, p.s_sto.n)
    p.se.wait_ge(p.s_sto2.h, p.s_sto2.n)
    p.bar()


def _new_nc():
    return bass.Bass("TRN2", target_bir_lowering=False)


def build_fused(w):
    nc = _new_nc()
    g = declare_dram(nc, w, ALL_NAMES)
    x_in = nc.dram_tensor("x_in", [NC, P, S], F32, kind="ExternalInput").ap()
    x_out = nc.dram_tensor("x_out", [NC, P, S], F32, kind="ExternalOutput").ap()
    xa = nc.dram_tensor("x_a", [NC, P, S], F32).ap()
    xb = nc.dram_tensor("x_b", [NC, P, S], F32).ap()
    cache = make_ffn0_cache(nc)
    scr = make_scr(nc)
    wc = make_wc(nc)
    with ExitStack() as ctx:
        p = Prog(nc, ctx)
        c = emit_const_load(p, g)
        emit_weight_prep(p, g, wc)
        emit_ffn0_dequant(p, g, cache)
        emit_ffn0_compute(p, c, g, cache, x_in, xa)
        p.push()
        emit_ln(p, c, xa, None, 0, S, y32_dram=scr["scr_y32"])
        p.bar()
        emit_att0_heads(p, c, g, scr["scr_y32"], scr["scr_z"])
        p.pop()
        p.bar()
        emit_oproj_residual(p, c, wc["wo_a0"], scr["scr_z"], xa, xb)
        p.bar()
        emit_ffn0_compute(p, c, g, cache, xb, xa)
        emit_ffn1(p, c, g, scr, wc, xa, xb)
        p.push()
        y = p.sb([P, NC * S], BF16)
        emit_ln(p, c, xb, y, 0, S)
        emit_att_softmax_heads(p, c, wc["qkv_a1"], y, scr["scr_z"])
        p.pop()
        p.bar()
        emit_oproj_residual(p, c, wc["wo_a1"], scr["scr_z"], xb, xa)
        p.bar()
        emit_ffn1(p, c, g, scr, wc, xa, x_out)
        finish(p)
    return nc


def kernel(**inputs):
    w = prep_weights(inputs)
    xs = [prep_x(inputs["L0_attn_x"], b) for b in range(8)]
    nc = build_fused(w)
    in_maps = []
    for b in range(8):
        m = {n: w[n] for n in ALL_NAMES}
        m["x_in"] = xs[b]
        in_maps.append(m)
    res = run_bass_kernel_spmd(nc, in_maps, core_ids=list(range(8)))
    return np.stack([post_out(np.asarray(res.results[b]["x_out"]))
                     for b in range(8)]).astype(np.float32)
